# version 19
# baseline (speedup 1.0000x reference)
import numpy as np
import ml_dtypes
import concourse.bass as bass
import concourse.mybir as mybir
from concourse.bass_utils import run_bass_kernel_spmd

F32 = mybir.dt.float32
BF16 = mybir.dt.bfloat16
AF = mybir.ActivationFunctionType
ALU = mybir.AluOpType
AX = mybir.AxisListType

B_TOT, T, D = 16, 4096, 1024
H, DK, SEG = 16, 64, 256
NCH = T // SEG
NCORES = 8
EPS_LN = 1e-5
EPS_RMS = 1e-6
EPS_DEN = 1e-6
SCALE = 1.0 / 32.0

ENGINES = ("pe", "dve", "act", "pool", "sp")
ENG2BLK = {"pe": "tensor", "dve": "vector", "act": "scalar", "pool": "gpsimd",
           "sp": "sync"}
SELF_WAIT = True
FULL_SERIAL = False
SERIAL_ENGINES = set()
PE_WAIT_DMA = False
PE_STALL_EVERY = 0


class Prog:

    def __init__(self):
        self.streams = {e: [] for e in ENGINES}
        self.count = {e: 0 for e in ENGINES}
        self.dma_count = {"spd": 0, "gpd": 0, "spn": 0, "gpn": 0}
        self.barrier_count = {"spv": 0, "gpv": 0}
        self._sealed = False
        self.reg_w = {}
        self.reg_r = {}
        self.hwm = {e: {} for e in ENGINES}
        self.n_ops = 0

    def _wait(self, stream, semkey, val):
        if val <= self.hwm[stream].get(semkey, 0):
            return
        self.streams[stream].append(("wait", semkey, val))
        self.hwm[stream][semkey] = val

    def _edges(self, stream, reads, writes):
        deps = {}
        for r in reads:
            w = self.reg_w.get(r)
            if w is not None:
                deps[w[0]] = max(deps.get(w[0], -1), w[1])
        for w_ in writes:
            w = self.reg_w.get(w_)
            if w is not None:
                deps[w[0]] = max(deps.get(w[0], -1), w[1])
            for rr in self.reg_r.get(w_, ()):
                deps[rr[0]] = max(deps.get(rr[0], -1), rr[1])
        for semkey, val in deps.items():
            if semkey == stream:
                if stream == "pe" or not SELF_WAIT:
                    continue
                if val >= self.count[stream] + 1:
                    raise RuntimeError("self-wait on future")
            self._wait(stream, semkey, val)

    def op(self, stream, fn, reads=(), writes=(), tag=""):
        assert stream in ("pe", "dve", "act", "pool")
        if stream == "pe" and PE_WAIT_DMA:
            for dk in ("spd", "gpd"):
                if self.dma_count[dk] > 0:
                    self._wait(stream, dk, self.dma_count[dk])
        if stream == "pe" and PE_STALL_EVERY:
            self._pe_n = getattr(self, "_pe_n", 0) + 1
            if self._pe_n % PE_STALL_EVERY == 0:
                for e2 in ("dve", "act"):
                    if self.count.get(e2, 0) > 0:
                        self._wait(stream, e2, self.count[e2])
        if FULL_SERIAL or stream in SERIAL_ENGINES or SERIAL_ENGINES & set(ENGINES):
            for e2 in ENGINES:
                if e2 == stream:
                    continue
                if (FULL_SERIAL or stream in SERIAL_ENGINES or e2 in SERIAL_ENGINES) \
                        and self.count.get(e2, 0) > 0:
                    self._wait(stream, e2, self.count[e2])
            if FULL_SERIAL or stream in SERIAL_ENGINES:
                for dk in ("spd", "gpd"):
                    if self.dma_count[dk] > 0:
                        self._wait(stream, dk, self.dma_count[dk])
        self._edges(stream, reads, writes)
        self.count[stream] += 1
        cnt = self.count[stream]
        self.streams[stream].append(("op", fn, stream, 1, tag))
        for r in reads:
            self.reg_r.setdefault(r, []).append((stream, cnt))
        for w in writes:
            self.reg_w[w] = (stream, cnt)
            self.reg_r[w] = []
        self.n_ops += 1

    def dma(self, queue, out_ap, in_ap, reads=(), writes=(), tag="",
            narrow=False):
        stream = "sp" if queue == "sp" else "pool"
        if narrow:
            assert not self._sealed
            semkey = "spn" if queue == "sp" else "gpn"
            mark = None
        else:
            semkey = "spd" if queue == "sp" else "gpd"
            viskey = "spv" if queue == "sp" else "gpv"
            mark = (viskey, self.barrier_count[viskey] + 1)
        self._edges(stream, reads, writes)
        self.dma_count[semkey] += 16
        cnt = self.dma_count[semkey]

        def fn(eng, o=out_ap, i=in_ap):
            return eng.dma_start(o, i)
        self.streams[stream].append(("op", fn, semkey, 16, tag))
        if mark is None:
            mark = (semkey, cnt)
        for r in reads:
            self.reg_r.setdefault(r, []).append(mark)
        for w in writes:
            self.reg_w[w] = mark
            self.reg_r[w] = []
        self.n_ops += 1

    def barrier(self, queue):
        stream = "sp" if queue == "sp" else "pool"
        semkey = "spd" if queue == "sp" else "gpd"
        viskey = "spv" if queue == "sp" else "gpv"
        self._wait(stream, semkey, self.dma_count[semkey])

        def fn(eng):
            return eng.nop()
        self.streams[stream].append(("op", fn, viskey, 1, "barrier"))
        self.barrier_count[viskey] += 1

    def seal_narrow(self):
        self._sealed = True
        for r, w in list(self.reg_w.items()):
            if w[0] in ("spn", "gpn"):
                self.reg_w[r] = (w[0], self.dma_count[w[0]])

    def final_store_wait(self):
        self._wait("sp", "spd", self.dma_count["spd"])
        self._wait("pool", "gpd", self.dma_count["gpd"])

    def emit(self, nc):
        sems = {}
        for s in list(ENGINES) + ["spd", "gpd", "spn", "gpn", "spv", "gpv"]:
            sems[s] = nc.alloc_semaphore(f"sem_{s}")
        blk = nc.Block("main")
        with blk:
            for e in ENGINES:
                items = self.streams[e]

                def body(eng, items=items):
                    for it in items:
                        if it[0] == "wait":
                            eng.wait_ge(sems[it[1]], it[2])
                        else:
                            _, fn, semkey, inc, tag = it
                            fn(eng).then_inc(sems[semkey], inc)
                getattr(blk, ENG2BLK[e])(body)


def build(nc, n_seq, n_chunks, dump=None, skip=()):
    p = Prog()
    gen = {}

    def mm_start(key):
        if not gen.get(key, False):
            gen[key] = True
            return True
        return False

    def gen_reset(*keys):
        for k in keys:
            gen[k] = False

    x_d = nc.dram_tensor("x", [n_seq, T, D], F32, kind="ExternalInput").ap()
    out_d = nc.dram_tensor("out", [n_seq, T, D], F32, kind="ExternalOutput").ap()
    w_d = {}
    for nme in ["wqt", "wkt", "wvt", "wut", "wot"]:
        w_d[nme] = nc.dram_tensor(nme, [D, D], F32, kind="ExternalInput").ap()
    cb_d = nc.dram_tensor("cb", [D, D], F32, kind="ExternalInput").ap()
    nw_d = nc.dram_tensor("nw", [128, 8], F32, kind="ExternalInput").ap()
    scb_d = nc.dram_tensor("scb", [128, 8], F32, kind="ExternalInput").ap()
    bias_d = nc.dram_tensor("bias", [1, D], F32, kind="ExternalInput").ap()

    ident_np = np.eye(128, dtype=np.float32).astype(ml_dtypes.bfloat16)
    atm = np.triu(np.ones((128, 128), np.float32))
    atmask_np = np.concatenate([atm, np.ones((128, 128), np.float32), atm],
                               axis=1).astype(ml_dtypes.bfloat16)
    dux0_np = np.tril(np.ones((128, 128), np.float32)).astype(ml_dtypes.bfloat16)
    dux1_np = np.concatenate([np.ones((128, 128), np.float32),
                              np.tril(np.ones((128, 128), np.float32))],
                             axis=1).astype(ml_dtypes.bfloat16)
    rc_np = np.stack([1.0 / np.arange(1, 129), 1.0 / np.arange(129, 257)],
                     axis=1).astype(np.float32)
    sel_np = np.zeros((16, 16, 128), np.float32)
    for nn in range(16):
        sel_np[nn, nn, :] = 1.0
    sel_np = sel_np.astype(ml_dtypes.bfloat16)
    ones_np = np.ones((1, 512), np.float32).astype(ml_dtypes.bfloat16)

    ident_d = nc.inline_tensor(np.ascontiguousarray(ident_np), "c_ident").ap()
    atmask_d = nc.inline_tensor(atmask_np, "c_atmask").ap()
    dux0_d = nc.inline_tensor(dux0_np, "c_dux0").ap()
    dux1_d = nc.inline_tensor(dux1_np, "c_dux1").ap()
    rc_d = nc.inline_tensor(rc_np, "c_recipct").ap()
    sel_d = nc.inline_tensor(np.ascontiguousarray(sel_np.reshape(16 * 16, 128)),
                             "c_sel").ap()
    ones_d = nc.inline_tensor(ones_np, "c_ones").ap()

    A = nc.alloc_sbuf_tensor
    wsb = {nme: A(nme + "_sb", [128, 8 * D], BF16) for nme in
           ["wqt", "wkt", "wvt", "wut", "wot"]}
    cbT = A("cbT_sb", [128, 8 * D], BF16)
    arch = A("arch", [128, 8 * 960], BF16)
    xs = [A(f"xbuf{i}", [128, 2 * D], F32) for i in range(2)]
    tm_sc = A("tm_sc", [128, 2 * D], BF16)
    xln_tm = tm_sc
    xn_tm = tm_sc
    xln_fm = A("xln_fm", [128, 2 * D], BF16)
    xn_fm = A("xn_fm", [128, 2 * D], BF16)
    y_tm = A("y_tm", [128, 2 * D], BF16)
    y_fm = A("y_fm", [128, 2 * D], BF16)
    u_fm = A("u_fm", [128, 2 * D], BF16)
    pq_fm = A("pq_fm", [128, 2 * D], BF16)
    pk_fm = A("pk_fm", [128, 2 * D], BF16)
    pk_tm = A("pk_tm", [128, 2 * D], BF16)
    v_tm = A("v_tm", [128, 2 * 16 * 65], BF16)
    eluE = A("eluE", [128, 512], BF16)
    eluR = A("eluR", [128, 512], BF16)
    sexcl = A("sexcl", [128, 8 * 65], BF16)
    zpad = A("zpad", [128, 16], BF16)
    at_sb = [A(f"atsb{i}", [128, 384], BF16) for i in range(2)]
    x_out = A("x_out", [128, 2 * D], F32)
    dxm = A("dxm", [128, 256], F32)
    gsb = A("gsb", [128, 8 * 256], BF16)
    bcsb = A("bcsb", [128, 2 * 256], BF16)
    gfm = A("gfm", [16, 256], BF16)
    gates = [A(f"gates{i}", [128, 17], F32) for i in range(2)]
    logits = [A(f"logits{i}", [128, 17], F32) for i in range(2)]
    gates_bf = A("gates_bf", [128, 16], BF16)
    mstk = A("mstk", [128, 8 * 16], BF16)
    msum = A("msum", [128, 8], F32)
    ST_NAMES = ["s1", "s2a", "s2b", "s2c", "m", "var", "u1", "rstd1", "sc2",
                "b1", "b2", "q2", "rr", "mx", "negb", "z1", "rz", "scr", "lnu",
                "lnq", "rcp", "tmp"]
    st = {nme: A("st_" + nme, [128, 4], F32) for nme in ST_NAMES}
    nmr_bf = A("nmr_bf", [128, 2], BF16)
    rr_bf = A("rr_bf", [128, 2], BF16)
    rr_row = A("rr_row", [1, 256], BF16)
    nmr_row = A("nmr_row", [1, 256], BF16)
    den_sb = A("den_sb", [128, 32], F32)
    rden = A("rden", [128, 32], F32)
    ident = A("identsb", [128, 128], BF16)
    atmask = A("atmasksb", [128, 384], BF16)
    dux0 = A("dux0sb", [128, 128], BF16)
    dux1 = A("dux1sb", [128, 256], BF16)
    rcct = A("rcctsb", [128, 2], F32)
    sel = A("selsb", [16, 16 * 128], BF16)
    ones_bf = A("onessb", [1, 512], BF16)
    bias_row = A("bias_row", [1, D], BF16)
    nw_sb = A("nw_sb", [128, 8], F32)
    scb_sb = A("scb_sb", [128, 8], F32)
    wstage = [A("wstage0", [128, D], F32)[:, :], A("wstage1", [128, D], F32)[:, :]]
    cbrow = A("cbrow", [128, D], BF16)
    dbg_d = None
    if dump is not None:
        dbg_d = nc.dram_tensor("dbg", [128, 2 * D], F32,
                               kind="ExternalOutput").ap()

    P = nc.alloc_psum_tensor
    s_psA = P("s_psA", [128, 4 * 65], F32)
    s_psB = P("s_psB", [128, 4 * 65], F32)
    pa = P("pa", [128, 512], F32)
    pb = P("pb", [128, 512], F32)
    tr = P("tr", [128, 512], BF16)
    at_ps = P("at_ps", [128, 384], F32)
    ybank = P("ybank", [128, 1024], F32)

    def emit_mm(psum_region, out_ap, lhsT, rhs, reads, key=None, tp=None,
                force_start=None, tag=""):
        k = key or psum_region
        strt = mm_start(k) if force_start is None else force_start

        def fn(eng, o=out_ap, l=lhsT, r=rhs, s=strt, t=tp):
            return eng.matmul(o, l, r, start=s, stop=True, tile_position=t)
        p.op("pe", fn, reads=reads, writes=(psum_region,), tag=tag)

    def transpose(psum_ap, in_ap, reads, writes, tag=""):
        def fn(eng, o=psum_ap, i=in_ap):
            return eng.transpose(o, i, ident[:, :])
        p.op("pe", fn, reads=tuple(reads) + ("ident",), writes=writes, tag=tag)

    def act(out_ap, in_ap, func, reads, writes, scale=1.0, bias=0.0,
            accum=None, tag=""):
        def fn(eng, o=out_ap, i=in_ap, f=func, s=scale, b=bias, a=accum):
            return eng.activation(o, i, f, bias=b, scale=s, accum_out=a)
        p.op("act", fn, reads=reads, writes=writes, tag=tag)

    def act_copy(out_ap, in_ap, reads, writes, tag=""):
        def fn(eng, o=out_ap, i=in_ap):
            return eng.copy(o, i)
        p.op("act", fn, reads=reads, writes=writes, tag=tag)

    def dve(method, out_ap, *args, reads=(), writes=(), tag=""):
        def fn(eng, m=method, o=out_ap, a=args):
            return getattr(eng, m)(o, *a)
        p.op("dve", fn, reads=reads, writes=writes, tag=tag)

    def pool_op(method, out_ap, *args, reads=(), writes=(), tag=""):
        def fn(eng, m=method, o=out_ap, a=args):
            return getattr(eng, m)(o, *a)
        p.op("pool", fn, reads=reads, writes=writes, tag=tag)

    def stt(stream, out_ap, in0, scal, in1, o0, o1, reads, writes, tag=""):
        def fn(eng, o=out_ap, a=in0, s=scal, b=in1, x=o0, y=o1):
            return eng.scalar_tensor_tensor(o, a, s, b, x, y)
        p.op(stream, fn, reads=reads, writes=writes, tag=tag)

    p.dma("sp", ident[:, :], ident_d, writes=("ident",), narrow=True)
    p.dma("sp", atmask[:, :], atmask_d, writes=("atmask",), narrow=True)
    p.dma("sp", dux0[:, :], dux0_d, writes=("dux0",), narrow=True)
    p.dma("sp", dux1[:, :], dux1_d, writes=("dux1",), narrow=True)
    p.dma("sp", rcct[:, :], rc_d, writes=("rcct",), narrow=True)
    for _kt in range(16):
        p.dma("sp", sel[:, _kt * 128:(_kt + 1) * 128],
              sel_d[_kt * 16:(_kt + 1) * 16, :], writes=("sel",), narrow=True)
    p.dma("sp", ones_bf[:, :], ones_d, writes=("ones",), narrow=True)
    p.dma("sp", nw_sb[:, :], nw_d, writes=("nw",), narrow=True)
    p.dma("sp", scb_sb[:, :], scb_d, writes=("scb",), narrow=True)
    p.dma("pool", bias_row[:, :], bias_d, writes=("bias_row",), narrow=True)
    dve("memset", v_tm[:, :], 1.0, writes=("v_tm",))
    dve("memset", arch[:, :], 0.0, writes=("arch",))
    dve("memset", zpad[:, :], 0.0, writes=("zpad",))

    def load_x(b, c, buf):
        for i in range(2):
            t0 = c * SEG + i * 128
            p.dma("sp", xs[buf][:, i * D:(i + 1) * D], x_d[b, t0:t0 + 128, :],
                  reads=(f"xb{buf}_{i}",), writes=(f"xb{buf}_{i}",),
                  tag=f"ldx{b}.{c}")

    load_x(0, 0, 0)

    for wi, nme in enumerate(["wqt", "wkt", "wvt"]):
        for kt in range(8):
            sb = (wi * 8 + kt) % 2
            p.dma("sp", wstage[sb], w_d[nme][kt * 128:(kt + 1) * 128, :],
                  reads=(f"wstage{sb}",), writes=(f"wstage{sb}",))
            if sb == 1:
                p.barrier("sp")
            dve("tensor_scalar_mul", wsb[nme][:, kt * D:(kt + 1) * D],
                wstage[sb], nw_sb[:, kt:kt + 1],
                reads=(f"wstage{sb}", "nw"), writes=(f"{nme}_k{kt}",))
    for nme in ["wut", "wot"]:
        for kt in range(8):
            p.dma("pool", wsb[nme][:, kt * D:(kt + 1) * D],
                  w_d[nme][kt * 128:(kt + 1) * 128, :],
                  writes=(f"{nme}_k{kt}",))
    p.barrier("pool")
    for nt in range(8):
        sb = nt % 2
        p.dma("sp", wstage[sb], cb_d[nt * 128:(nt + 1) * 128, :],
              reads=(f"wstage{sb}",), writes=(f"wstage{sb}",))
        p.barrier("sp")
        def cbs_fn(eng, o=cbrow[:, :], i=wstage[sb],
                   s1=scb_sb[:, nt:nt + 1]):
            return eng.tensor_scalar(o, i, s1, 1.0 / 127.0, ALU.mult, ALU.mult)
        p.op("dve", cbs_fn, reads=(f"wstage{sb}", "scb"), writes=("cbrow",))
        for k2 in (0, 4):
            for kk in range(4):
                kt = k2 + kk
                transpose(tr[:, kk * 128:(kk + 1) * 128],
                          cbrow[:, kt * 128:(kt + 1) * 128],
                          reads=("cbrow",), writes=("tr0", "tr1"))
            dst = cbT[:, :].rearrange("p (k g) -> p k g", k=8)[
                :, k2:k2 + 4, nt * 128:(nt + 1) * 128]
            src = tr[:, :].rearrange("p (k g) -> p k g", k=4)
            act_copy(dst, src, reads=("tr0", "tr1"), writes=("cbT",))

    def ln_stats2(x_aps, xregs, stc, eps, rms, sq_target, sq_region, pieces):
        c2 = (slice(None), slice(stc, stc + 2))
        accs = ["s2a", "s2b", "s2c"]
        for i in range(2):
            dve("tensor_reduce", st["s1"][:, stc + i:stc + i + 1], x_aps[i],
                AX.X, ALU.add, reads=(xregs[i],), writes=("st_s1",))
            off = 0
            for pi, plen in enumerate(pieces):
                act(sq_target[:, 0:plen], x_aps[i][:, off:off + plen],
                    AF.Square, reads=(xregs[i],), writes=(sq_region,),
                    accum=st[accs[pi]][:, stc + i:stc + i + 1])
                p.reg_w["st_" + accs[pi]] = p.reg_w[sq_region]
                p.reg_r["st_" + accs[pi]] = []
                off += plen
        dve("tensor_scalar_mul", st["m"][c2], st["s1"][c2], 1.0 / 1024.0,
            reads=("st_s1",), writes=("st_m",))
        dve("tensor_add", st["tmp"][c2], st["s2a"][c2], st["s2b"][c2],
            reads=("st_s2a", "st_s2b"), writes=("st_tmp",))
        if len(pieces) > 2:
            dve("tensor_add", st["tmp"][c2], st["tmp"][c2], st["s2c"][c2],
                reads=("st_tmp", "st_s2c"), writes=("st_tmp",))
        dve("tensor_mul", st["q2"][c2], st["m"][c2], st["m"][c2],
            reads=("st_m",), writes=("st_q2",))
        stt("dve", st["var"][c2], st["tmp"][c2], 1.0 / 1024.0, st["q2"][c2],
            ALU.mult, ALU.subtract, ("st_tmp", "st_q2"), ("st_var",))
        dve("tensor_scalar_add", st["u1"][c2], st["var"][c2], float(eps),
            reads=("st_var",), writes=("st_u1",))
        act(st["lnu"][c2], st["u1"][c2], AF.Ln,
            reads=("st_u1",), writes=("st_lnu",))
        act(st["rstd1"][c2], st["lnu"][c2], AF.Exp, scale=-0.5,
            reads=("st_lnu",), writes=("st_rstd1",))
        stt("dve", st["b1"][c2], st["m"][c2], -1.0, st["rstd1"][c2],
            ALU.mult, ALU.mult, ("st_m", "st_rstd1"), ("st_b1",))
        if rms:
            dve("tensor_mul", st["rcp"][c2], st["rstd1"][c2], st["rstd1"][c2],
                reads=("st_rstd1",), writes=("st_rcp",))
            def q2f(eng, o=st["q2"][c2], x=st["rcp"][c2]):
                return eng.tensor_scalar(o, x, -float(eps), 1.0 + EPS_RMS,
                                         ALU.mult, ALU.add)
            p.op("dve", q2f, reads=("st_rcp",), writes=("st_q2",))
            act(st["lnq"][c2], st["q2"][c2], AF.Ln, reads=("st_q2",),
                writes=("st_lnq",))
            act(st["rr"][c2], st["lnq"][c2], AF.Exp, scale=-0.5,
                reads=("st_lnq",), writes=("st_rr",))
            dve("tensor_mul", st["sc2"][c2], st["rstd1"][c2], st["rr"][c2],
                reads=("st_rstd1", "st_rr"), writes=("st_sc2",))
            stt("dve", st["b2"][c2], st["m"][c2], -1.0, st["sc2"][c2],
                ALU.mult, ALU.mult, ("st_m", "st_sc2"), ("st_b2",))

    def transpose_set(src, src_region, dst, dst_region, direction,
                      accum_ms=False):
        if direction == "tm2fm":
            tiles = [(i * 1024 + j * 128, j * 256 + i * 128)
                     for j in range(8) for i in range(2)]
        else:
            tiles = [(g * 256 + i * 128, i * 1024 + g * 128)
                     for i in range(2) for g in range(8)]
        for half in range(4):
            for kk in range(4):
                s_off, _ = tiles[half * 4 + kk]
                transpose(tr[:, kk * 128:(kk + 1) * 128],
                          src[:, s_off:s_off + 128],
                          reads=(src_region,), writes=("tr0", "tr1"))
            if accum_ms:
                for jj in range(2):
                    j = half * 2 + jj
                    d_off = tiles[half * 4 + jj * 2][1]
                    act(dst[:, d_off:d_off + 256],
                        tr[:, jj * 256:(jj + 1) * 256], AF.Copy,
                        reads=("tr0", "tr1"), writes=(dst_region,),
                        accum=msum[:, j:j + 1])
                    p.reg_w[f"msum_{j}"] = p.reg_w[dst_region]
                    p.reg_r[f"msum_{j}"] = []
            else:
                d_off = tiles[half * 4][1]
                if half % 2 == 0:
                    act_copy(dst[:, d_off:d_off + 512], tr[:, 0:512],
                             reads=("tr0", "tr1"), writes=(dst_region,))
                else:
                    dve("tensor_copy", dst[:, d_off:d_off + 512], tr[:, 0:512],
                        reads=("tr0", "tr1"), writes=(dst_region,))
            yield

    def proj_fm(wname, epilogue, src_fm, src_region):
        for w in range(4):
            ps, psr = (pa, "pa") if w % 2 == 0 else (pb, "pb")
            gen_reset(psr)
            for half in range(2):
                ot = 2 * w + half
                for kt in range(8):
                    emit_mm(psr, ps[:, half * 256:(half + 1) * 256],
                            wsb[wname][:, kt * D + ot * 128:kt * D + (ot + 1) * 128],
                            src_fm[:, kt * 256:(kt + 1) * 256],
                            reads=(f"{wname}_k{kt}", src_region), key=psr)
            epilogue(w, ps, psr)
            yield

    def phase_A1(b, c):
        buf = c % 2
        xreg = (f"xb{buf}_0", f"xb{buf}_1")
        stc = (c % 2) * 2
        if c == 0:
            dve("memset", gfm[:, :], 0.0, writes=("gfm",))
            gen_reset("s_psA0", "s_psA64", "s_psB0", "s_psB64")
        p.barrier("sp")
        if c + 1 < n_chunks:
            load_x(b, c + 1, (c + 1) % 2)
        elif b + 1 < n_seq:
            load_x(b + 1, 0, (c + 1) % 2)
        x_aps = [xs[buf][:, i * D:(i + 1) * D] for i in range(2)]
        ln_stats2(x_aps, xreg, stc, EPS_LN, True, at_ps, "at_ps",
                  (384, 384, 256))
        yield
        for i in range(2):
            def aff(eng, o=xln_tm[:, i * D:(i + 1) * D], x=x_aps[i],
                    s1=st["rstd1"][:, stc + i:stc + i + 1],
                    s2=st["b1"][:, stc + i:stc + i + 1]):
                return eng.tensor_scalar(o, x, s1, s2, ALU.mult, ALU.add)
            p.op("dve", aff, reads=(xreg[i], "st_rstd1", "st_b1"),
                 writes=("tm_sc",))
            act_copy(nmr_bf[:, i:i + 1], st["b1"][:, stc + i:stc + i + 1],
                     reads=("st_b1",), writes=("nmr_bf",))
            yield

    def phase_A2(b, c):
        buf = c % 2
        xreg = (f"xb{buf}_0", f"xb{buf}_1")
        stc = (c % 2) * 2
        yield from transpose_set(xln_tm, "tm_sc", xln_fm, "xln_fm", "tm2fm",
                                 accum_ms=(c < n_chunks - 1))
        if c < n_chunks - 1:
            for j in range(8):
                dve("tensor_scalar_mul", mstk[:, j * 16 + c:j * 16 + c + 1],
                    msum[:, j:j + 1], 1.0 / 256.0,
                    reads=(f"msum_{j}",), writes=("mstk",))
        act_copy(rr_bf[:, 0:2], st["rr"][:, stc:stc + 2], reads=("st_rr",),
                 writes=("rr_bf",))
        for i in range(2):
            transpose(tr[0:1, i * 128:(i + 1) * 128], nmr_bf[:, i:i + 1],
                      reads=("nmr_bf",), writes=("tr0", "tr1"))
            act_copy(nmr_row[0:1, i * 128:(i + 1) * 128],
                     tr[0:1, i * 128:(i + 1) * 128],
                     reads=("tr0", "tr1"), writes=("nmr_row",))
            transpose(tr[0:1, 256 + i * 128:256 + (i + 1) * 128],
                      rr_bf[:, i:i + 1], reads=("rr_bf",),
                      writes=("tr0", "tr1"))
            act_copy(rr_row[0:1, i * 128:(i + 1) * 128],
                     tr[0:1, 256 + i * 128:256 + (i + 1) * 128],
                     reads=("tr0", "tr1"), writes=("rr_row",))
        gen_reset("pa")
        emit_mm("pa", pa[:, 0:256], ones_bf[0:1, 0:128], rr_row[0:1, 0:256],
                reads=("ones", "rr_row"), key="pa")
        xl3 = xln_fm[:, :].rearrange("p (g t) -> p g t", g=8)
        bc3 = pa[:, 0:256].unsqueeze(1).broadcast_to([128, 8, 256])
        dve("tensor_mul", xn_fm[:, :].rearrange("p (g t) -> p g t", g=8),
            xl3, bc3, reads=("xln_fm", "pa"), writes=("xn_fm",))
        yield

    def phase_B_gen(b, c):
        yield from _phase_B(b, c)

    def phase_B(b, c):
        for _ in _phase_B(b, c):
            pass

    def _phase_B(b, c):
        stc = (c % 2) * 2

        def u_epi(w, ps, psr):
            act_copy(u_fm[:, w * 512:(w + 1) * 512], ps[:, :], reads=(psr,),
                     writes=("u_fm",))

        def elu_epi_for(dst, dstr):
            def elu_epi(w, ps, psr):
                act(eluE[:, :], ps[:, :], AF.Exp, reads=(psr,),
                    writes=("eluE",))
                act(eluR[:, :], ps[:, :], AF.Relu, reads=(psr,),
                    writes=("eluR",))
                stt("dve", dst[:, w * 512:(w + 1) * 512], eluE[:, :], 1.0,
                    eluR[:, :], ALU.min, ALU.add, ("eluE", "eluR"), (dstr,))
            return elu_epi

        yield from proj_fm("wut", u_epi, xln_fm, "xln_fm")
        yield from proj_fm("wqt", elu_epi_for(pq_fm, "pq_fm"), xn_fm, "xn_fm")
        yield from proj_fm("wkt", elu_epi_for(pk_fm, "pk_fm"), xn_fm, "xn_fm")
        for i in range(2):
            for half in range(2):
                ps, psr = (pa, "pa") if half == 0 else (pb, "pb")
                gen_reset(psr)
                for kt in range(8):
                    emit_mm(psr, ps[:, :],
                            xn_fm[:, kt * 256 + i * 128:kt * 256 + (i + 1) * 128],
                            wsb["wvt"][:, kt * D + half * 512:kt * D + (half + 1) * 512],
                            reads=(f"wvt_k{kt}", "xn_fm"), key=psr)
                dst = v_tm[:, :].rearrange("p (ii h vv) -> p ii h vv",
                                           ii=2, h=16)[:, i,
                                                       half * 8:(half + 1) * 8,
                                                       0:64]
                src = ps[:, :].rearrange("p (h vv) -> p h vv", h=8)
                act_copy(dst, src, reads=(psr,), writes=("v_tm",))
                yield
        for i in range(2):
            ps, psr = (pa, "pa") if i == 0 else (pb, "pb")
            gen_reset(psr)
            ncols = 128 if i == 0 else 256
            for kt in range(8):
                emit_mm(psr, ps[:, 0:ncols],
                        u_fm[:, kt * 256 + i * 128:kt * 256 + (i + 1) * 128],
                        xln_fm[:, kt * 256:kt * 256 + ncols],
                        reads=("u_fm", "xln_fm"), key=psr)
            mask, mr = (dux0, "dux0") if i == 0 else (dux1, "dux1")
            dve("tensor_mul", dxm[:, 0:ncols], ps[:, 0:ncols],
                mask[:, 0:ncols], reads=(psr, mr), writes=("dxm",))
            dve("tensor_reduce", st["mx"][:, stc + i:stc + i + 1],
                dxm[:, 0:ncols], AX.X, ALU.add, reads=("dxm",),
                writes=("st_mx",))
            dve("tensor_scalar_mul", logits[i][:, c:c + 1],
                st["mx"][:, stc + i:stc + i + 1], rcct[:, i:i + 1],
                reads=("st_mx", "rcct"), writes=(f"logits{i}",))
            yield
        if c > 0:
            for i in range(2):
                ps, psr = (pa, "pa") if i == 0 else (pb, "pb")
                gen_reset(psr)
                for kt in range(8):
                    emit_mm(psr, ps[:, 0:c],
                            u_fm[:, kt * 256 + i * 128:kt * 256 + (i + 1) * 128],
                            mstk[:, kt * 16:kt * 16 + c],
                            reads=("u_fm", "mstk"), key=psr)
                act_copy(logits[i][:, 0:c], ps[:, 0:c], reads=(psr,),
                         writes=(f"logits{i}",))
        for i in range(2):
            lr, gr = f"logits{i}", f"gates{i}"
            ci = (slice(None), slice(stc + i, stc + i + 1))
            dve("tensor_reduce", st["mx"][ci], logits[i][:, 0:c + 1],
                AX.X, ALU.max, reads=(lr,), writes=("st_mx",))
            dve("tensor_scalar_mul", st["negb"][ci], st["mx"][ci], -SCALE,
                reads=("st_mx",), writes=("st_negb",))
            act(gates[i][:, 0:c + 1], logits[i][:, 0:c + 1], AF.Exp,
                scale=SCALE, bias=st["negb"][ci],
                reads=(lr, "st_negb"), writes=(gr,))
            dve("tensor_reduce", st["z1"][ci], gates[i][:, 0:c + 1],
                AX.X, ALU.add, reads=(gr,), writes=("st_z1",))
            dve("reciprocal", st["rz"][ci], st["z1"][ci],
                reads=("st_z1",), writes=("st_rz",))
            dve("tensor_scalar_mul", gates[i][:, 0:c + 1],
                gates[i][:, 0:c + 1], st["rz"][ci],
                reads=(gr, "st_rz"), writes=(gr,))

    def phase_C(b, c):
        if c > 0:
            act_copy(sexcl[:, 0:260], s_psA[:, :], reads=("s_psA",),
                     writes=("sexcl",))
            act_copy(sexcl[:, 260:520], s_psB[:, :], reads=("s_psB",),
                     writes=("sexcl",))
            for (sps, spr, h0) in ((s_psA, "s_psA", 0), (s_psB, "s_psB", 8)):
                for rw in (0, 64):
                    src_ap = sps[rw:rw + 64, :].rearrange(
                        "p (g q) -> p g q", g=4)[:, :, 64:65]
                    dst_ap = zpad[rw:rw + 64, h0 + rw // 64:h0 + 8:2].unsqueeze(2)
                    act_copy(dst_ap, src_ap, reads=(spr,), writes=("zpad",))
        yield
        last = c >= n_chunks - 1
        for gp in range(4):
            for kk, (g, i) in enumerate([(2 * gp, 0), (2 * gp, 1),
                                         (2 * gp + 1, 0), (2 * gp + 1, 1)]):
                transpose(tr[:, kk * 128:(kk + 1) * 128],
                          pk_fm[:, g * 256 + i * 128:g * 256 + (i + 1) * 128],
                          reads=("pk_fm",), writes=("tr0", "tr1"))
            act_copy(pk_tm[:, 2 * gp * 128:2 * gp * 128 + 256
                           ].rearrange("p (k f) -> p k f", k=2),
                     tr[:, :].rearrange("p (k f) -> p k f", k=4)[:, 0::2, :],
                     reads=("tr0", "tr1"), writes=("pk_tm",))
            dve("tensor_copy",
                pk_tm[:, 1024 + 2 * gp * 128:1024 + 2 * gp * 128 + 256
                      ].rearrange("p (k f) -> p k f", k=2),
                tr[:, :].rearrange("p (k f) -> p k f", k=4)[:, 1::2, :],
                reads=("tr0", "tr1"), writes=("pk_tm",))
            yield
            if not last:
                for h in range(4 * gp, 4 * gp + 4):
                    hp = h // 2
                    sps, spr = (s_psA, "s_psA") if hp < 4 else (s_psB, "s_psB")
                    col = (hp % 4) * 65
                    rw = (h % 2) * 64
                    for i in range(2):
                        emit_mm(spr, sps[rw:rw + 64, col:col + 65],
                                pk_tm[:, i * D + h * 64:i * D + (h + 1) * 64],
                                v_tm[:, i * 1040 + h * 65:i * 1040 + (h + 1) * 65],
                                reads=("pk_tm", "v_tm"), key=f"{spr}{rw}",
                                tp=(0, rw))
                    if h % 2 == 1:
                        yield
        if not last:
            for (sps, spr, hp0) in ((s_psA, "s_psA", 0), (s_psB, "s_psB", 4)):
                src_ap = sps[:, :].rearrange("p (g q) -> p g q", g=4)[:, :, 0:64]
                dst_ap = arch[:, :].rearrange("p (g q) -> p g q", g=8)[
                    :, hp0:hp0 + 4, c * 64:(c + 1) * 64]
                act_copy(dst_ap, src_ap, reads=(spr,), writes=("arch",))
            yield

    def phase_D(b, c):
        gen_reset("den")

        def emit_m4(h):
            hp = h // 2
            rw = (h % 2) * 64
            aps, apr = (at_ps, "at_ps") if h % 2 == 0 else (pb, "pb")
            gen_reset(apr)
            emit_mm(apr, aps[:, 0:256],
                    pk_fm[rw:rw + 64, hp * 256:hp * 256 + 128],
                    pq_fm[rw:rw + 64, hp * 256:(hp + 1) * 256],
                    reads=("pk_fm", "pq_fm"), key=apr, tp=(rw, 0))
            emit_mm(apr, aps[:, 256:384],
                    pk_fm[rw:rw + 64, hp * 256 + 128:(hp + 1) * 256],
                    pq_fm[rw:rw + 64, hp * 256 + 128:(hp + 1) * 256],
                    reads=("pk_fm", "pq_fm"), key=apr, tp=(rw, 0))

        emit_m4(0)
        for hhalf in range(2):
            gen_reset("ybA", "ybB")
            for h8 in range(8):
                h = hhalf * 8 + h8
                if h < 15:
                    emit_m4(h + 1)
                hp = h // 2
                rw = (h % 2) * 64
                slot = at_sb[h % 2]
                slotr = f"atsb{h % 2}"
                aps, apr = (at_ps, "at_ps") if h % 2 == 0 else (pb, "pb")
                dve("tensor_mul", slot[:, 0:128], aps[:, 0:128],
                    atmask[:, 0:128], reads=(apr, "atmask"), writes=(slotr,))
                dve("tensor_mul", slot[:, 128:384], aps[:, 128:384],
                    atmask[:, 128:384], reads=(apr, "atmask"),
                    writes=(slotr + "x",))
                par = h8 % 2
                ybr = "ybA" if par == 0 else "ybB"
                ktiles = {0: [(0, slice(0, 128))],
                          1: [(0, slice(128, 256)), (1, slice(256, 384))]}
                for i in range(2):
                    ycol = par * 512 + i * 256 + (h8 // 2) * 64
                    for (dt, sl) in ktiles[i]:
                        slr = slotr if sl.start == 0 else slotr + "x"
                        emit_mm("ybank", ybank[:, ycol:ycol + 64],
                                slot[:, sl],
                                v_tm[:, dt * 1040 + h * 65:dt * 1040 + h * 65 + 64],
                                reads=(slr, "v_tm"), key=ybr)
                        emit_mm("pa", pa[:, i * 16 + h:i * 16 + h + 1],
                                slot[:, sl],
                                v_tm[:, dt * 1040 + h * 65 + 64:dt * 1040 + h * 65 + 65],
                                reads=(slr, "v_tm"), key="den")
                    if c > 0:
                        sbase = (hp % 4) * 65 + 260 * (hp // 4)
                        emit_mm("ybank", ybank[:, ycol:ycol + 64],
                                pq_fm[rw:rw + 64, hp * 256 + i * 128:hp * 256 + (i + 1) * 128],
                                sexcl[rw:rw + 64, sbase:sbase + 64],
                                reads=("pq_fm", "sexcl"), key=ybr,
                                tp=(rw, 0))
                        emit_mm("pa", pa[:, i * 16 + h:i * 16 + h + 1],
                                pq_fm[:, hp * 256 + i * 128:hp * 256 + (i + 1) * 128],
                                zpad[:, h:h + 1],
                                reads=("pq_fm", "zpad"), key="den")
                yield
            for i in range(2):
                cl = i * 16 + hhalf * 8
                act(den_sb[:, cl:cl + 8], pa[:, cl:cl + 8], AF.Copy,
                    bias=EPS_DEN, reads=("pa",), writes=("den_sb",))
                dve("reciprocal", rden[:, cl:cl + 8], den_sb[:, cl:cl + 8],
                    reads=("den_sb",), writes=("rden",))
                for par in range(2):
                    y3 = ybank[:, par * 512 + i * 256:par * 512 + (i + 1) * 256
                               ].rearrange("p (h vv) -> p h vv", h=4)
                    rv = rden[:, cl + par:cl + 8:2].unsqueeze(2).broadcast_to(
                        [128, 4, 64])
                    dst = y_tm[:, i * D + hhalf * 512:i * D + (hhalf + 1) * 512
                               ].rearrange("p (h vv) -> p h vv", h=8)[:, par::2, :]
                    dve("tensor_mul", dst, y3, rv,
                        reads=("ybank", "rden"), writes=("y_tm",))
            tiles = [(i * 1024 + j * 128, j * 256 + i * 128)
                     for j in range(hhalf * 4, hhalf * 4 + 4)
                     for i in range(2)]
            for half in range(2):
                for kk in range(4):
                    s_off, _ = tiles[half * 4 + kk]
                    transpose(tr[:, kk * 128:(kk + 1) * 128],
                              y_tm[:, s_off:s_off + 128],
                              reads=("y_tm",), writes=("tr0", "tr1"))
                d_off = tiles[half * 4][1]
                if half == 0:
                    act_copy(y_fm[:, d_off:d_off + 512], tr[:, 0:512],
                             reads=("tr0", "tr1"), writes=("y_fm",))
                else:
                    dve("tensor_copy", y_fm[:, d_off:d_off + 512],
                        tr[:, 0:512], reads=("tr0", "tr1"), writes=("y_fm",))
                yield

    def phase_E(b, c):
        buf = c % 2
        xreg = (f"xb{buf}_0", f"xb{buf}_1")
        stc = (c % 2) * 2
        for i in range(2):
            for half in range(2):
                ps, psr = (pa, "pa") if half == 0 else (pb, "pb")
                gen_reset(psr)
                for kt in range(8):
                    emit_mm(psr, ps[:, :],
                            y_fm[:, kt * 256 + i * 128:kt * 256 + (i + 1) * 128],
                            wsb["wot"][:, kt * D + half * 512:kt * D + (half + 1) * 512],
                            reads=(f"wot_k{kt}", "y_fm"), key=psr)
                emit_mm(psr, ps[:, :],
                        nmr_row[0:1, i * 128:(i + 1) * 128],
                        ones_bf[0:1, 0:512],
                        reads=("nmr_row", "ones"), key=psr)
                stt("dve", x_out[:, i * D + half * 512:i * D + (half + 1) * 512],
                    xs[buf][:, i * D + half * 512:i * D + (half + 1) * 512],
                    st["rstd1"][:, stc + i:stc + i + 1], ps[:, :],
                    ALU.mult, ALU.add,
                    (xreg[i], "st_rstd1", psr), ("x_out",))
                yield

    def phase_F(b, c):
        buf = c % 2
        xreg = (f"xb{buf}_0", f"xb{buf}_1")
        if c > 0:
            for i in range(2):
                act_copy(gates_bf[:, 0:c], gates[i][:, 0:c],
                         reads=(f"gates{i}",), writes=("gates_bf",))
                transpose(tr[0:c, i * 128:(i + 1) * 128], gates_bf[:, 0:c],
                          reads=("gates_bf",), writes=("tr0",))
                act_copy(gfm[0:c, i * 128:(i + 1) * 128],
                         tr[0:c, i * 128:(i + 1) * 128],
                         reads=("tr0",), writes=("gfm",))
            yield
            for hhalf in range(2):
                gen_reset("ybA", "ybB")

                def emit_bc_gtt(nn):
                    bslot = nn % 2
                    gen_reset("pa")
                    emit_mm("pa", pa[:, 0:256],
                            sel[:, nn * 128:(nn + 1) * 128], gfm[:, :],
                            reads=("sel", "gfm"), key="pa")
                    act_copy(bcsb[:, bslot * 256:(bslot + 1) * 256],
                             pa[:, 0:256], reads=("pa",),
                             writes=(f"bcsb{bslot}",))
                    for hp4 in range(4):
                        hp = hhalf * 4 + hp4
                        gslot = bslot * 4 + hp4
                        dve("tensor_mul",
                            gsb[:, gslot * 256:(gslot + 1) * 256],
                            pq_fm[:, hp * 256:(hp + 1) * 256],
                            bcsb[:, bslot * 256:(bslot + 1) * 256],
                            reads=("pq_fm", f"bcsb{bslot}"),
                            writes=(f"gsb{gslot}",))

                emit_bc_gtt(0)
                for nn in range(c):
                    if nn + 1 < c:
                        emit_bc_gtt(nn + 1)
                    bslot = nn % 2
                    for i in range(2):
                        for h8 in range(8):
                            h = hhalf * 8 + h8
                            hp4 = h8 // 2
                            hp = hhalf * 4 + hp4
                            gslot = bslot * 4 + hp4
                            rw = (h % 2) * 64
                            par = h8 % 2
                            ybr = "ybA" if par == 0 else "ybB"
                            ycol = par * 512 + i * 256 + (h8 // 2) * 64
                            emit_mm("ybank",
                                    ybank[:, ycol:ycol + 64],
                                    gsb[rw:rw + 64,
                                        gslot * 256 + i * 128:gslot * 256 + (i + 1) * 128],
                                    arch[rw:rw + 64,
                                         hp * 960 + nn * 64:hp * 960 + (nn + 1) * 64],
                                    reads=(f"gsb{gslot}", "arch"), key=ybr,
                                    tp=(rw, 0))
                    yield
                for i in range(2):
                    for par in range(2):
                        xo3 = x_out[:, i * D + hhalf * 512:i * D + (hhalf + 1) * 512
                                    ].rearrange("p (h vv) -> p h vv", h=8)[:, par::2, :]
                        yb3 = ybank[:, par * 512 + i * 256:par * 512 + (i + 1) * 256
                                    ].rearrange("p (h vv) -> p h vv", h=4)
                        stt("dve", xo3, xo3, gates[i][:, c:c + 1], yb3,
                            ALU.mult, ALU.add,
                            ("x_out", f"gates{i}", "ybank"), ("x_out",))
        else:
            for i in range(2):
                dve("tensor_scalar_mul", x_out[:, i * D:(i + 1) * D],
                    x_out[:, i * D:(i + 1) * D], gates[i][:, c:c + 1],
                    reads=("x_out", f"gates{i}"), writes=("x_out",))
        for i in range(2):
            pool_op("tensor_add", x_out[:, i * D:(i + 1) * D],
                    x_out[:, i * D:(i + 1) * D],
                    xs[buf][:, i * D:(i + 1) * D],
                    reads=("x_out", xreg[i]), writes=("x_out",))
        yield

    def phase_G2(b, c):
        stc = (c % 2) * 2
        ln_stats2([x_out[:, i * D:(i + 1) * D] for i in range(2)],
                  ("x_out", "x_out"), stc, EPS_LN, False, ybank, "ybank",
                  (512, 512))
        for i in range(2):
            def aff3(eng, o=y_tm[:, i * D:(i + 1) * D],
                     x=x_out[:, i * D:(i + 1) * D],
                     s1=st["rstd1"][:, stc + i:stc + i + 1],
                     s2=st["b1"][:, stc + i:stc + i + 1]):
                return eng.tensor_scalar(o, x, s1, s2, ALU.mult, ALU.add)
            p.op("dve", aff3, reads=("x_out", "st_rstd1", "st_b1"),
                 writes=("y_tm",))
        yield
        yield from transpose_set(y_tm, "y_tm", y_fm, "y_fm", "tm2fm")
        for i in range(2):
            for half in range(2):
                ps = ybank[:, half * 512:(half + 1) * 512]
                key = "ybA" if half == 0 else "ybB"
                gen_reset(key)
                for kt in range(8):
                    emit_mm("ybank", ps,
                            y_fm[:, kt * 256 + i * 128:kt * 256 + (i + 1) * 128],
                            cbT[:, kt * D + half * 512:kt * D + (half + 1) * 512],
                            reads=("cbT", "y_fm"), key=key)
                emit_mm("ybank", ps, ones_bf[0:1, 0:128],
                        bias_row[0:1, half * 512:(half + 1) * 512],
                        reads=("ones", "bias_row"), key=key)
                dve("tensor_add",
                    x_out[:, i * D + half * 512:i * D + (half + 1) * 512],
                    x_out[:, i * D + half * 512:i * D + (half + 1) * 512],
                    ps, reads=("x_out", "ybank"), writes=("x_out",))
                yield
        if dump is not None and dump[1] == (b, c):
            nm = dump[0]
            tens = {"xln_fm": xln_fm, "xn_fm": xn_fm, "u_fm": u_fm,
                    "pq_fm": pq_fm, "pk_fm": pk_fm, "pk_tm": pk_tm,
                    "v_tm": v_tm, "y_tm": y_tm, "y_fm": y_fm,
                    "x_out": x_out, "bs": x_out, "arch": arch,
                    "sexcl": sexcl, "mstk": mstk, "zpad": zpad,
                    "gfm": gfm, "den_sb": den_sb, "rden": rden,
                    "logits0": logits[0], "gates0": gates[0]}[nm]
            ncol = min(tens.shape[1], 2048)
            nrow = tens.shape[0]
            p.dma("pool", dbg_d[0:nrow, 0:ncol], tens[:, 0:ncol],
                  reads=(), writes=())
        for i in range(2):
            t0 = c * SEG + i * 128
            p.dma("pool", out_d[b, t0:t0 + 128, :],
                  x_out[:, i * D:(i + 1) * D],
                  reads=("x_out",), writes=(), tag=f"st{b}.{c}")
        p.barrier("pool")

    def run_gen(g):
        if g is None:
            return
        for _ in g:
            pass

    def inter(g1, g2, r1=1, r2=1):
        gens = [(g, r) for g, r in ((g1, r1), (g2, r2)) if g is not None]
        while gens:
            nxt_gens = []
            for g, r in gens:
                alive = True
                for _ in range(r):
                    try:
                        next(g)
                    except StopIteration:
                        alive = False
                        break
                if alive:
                    nxt_gens.append((g, r))
            gens = nxt_gens

    p.seal_narrow()
    coords = [(b, c) for b in range(n_seq) for c in range(n_chunks)]
    run_gen(phase_A1(*coords[0]))
    run_gen(phase_A2(*coords[0]))
    phase_B(*coords[0])
    for idx, (b, c) in enumerate(coords):
        nxt = coords[idx + 1] if idx + 1 < len(coords) else None
        inter(phase_C(b, c), phase_D(b, c))
        inter(phase_E(b, c), phase_A1(*nxt) if nxt else None)
        inter(phase_F(b, c), phase_A2(*nxt) if nxt else None, r1=2, r2=1)
        inter(phase_B_gen(*nxt) if nxt else None, phase_G2(b, c), r1=2, r2=1)

    p.final_store_wait()
    p.emit(nc)
    return p


def kernel(L0_attn_x, L0_attn_norm_w, L0_attn_Wq, L0_attn_Wk, L0_attn_Wv,
           L0_attn_Wo, L0_attn_Wu, L0_ffn_CB, L0_ffn_SCB, L0_ffn_bias,
           _n_chunks=NCH, _n_seq=2, _dump=None, _skip=()):
    x = np.ascontiguousarray(L0_attn_x, dtype=np.float32)
    wqt = np.ascontiguousarray(L0_attn_Wq.T, dtype=np.float32)
    wkt = np.ascontiguousarray(L0_attn_Wk.T, dtype=np.float32)
    wvt = np.ascontiguousarray(L0_attn_Wv.T, dtype=np.float32)
    wut = np.ascontiguousarray(L0_attn_Wu.T, dtype=np.float32)
    wot = np.ascontiguousarray(L0_attn_Wo.T, dtype=np.float32)
    cb = np.ascontiguousarray(L0_ffn_CB, dtype=np.float32)
    nw_t = np.ascontiguousarray(
        np.asarray(L0_attn_norm_w, np.float32).reshape(8, 128).T)
    scb_t = np.ascontiguousarray(
        np.asarray(L0_ffn_SCB, np.float32).reshape(8, 128).T)
    bias_r = np.ascontiguousarray(
        np.asarray(L0_ffn_bias, np.float32).reshape(1, D))

    nc = bass.Bass("TRN2", target_bir_lowering=False)
    build(nc, _n_seq, _n_chunks, dump=_dump, skip=_skip)

    in_maps = []
    for core in range(NCORES):
        im = {"x": np.ascontiguousarray(x[core * 2:core * 2 + _n_seq]),
              "wqt": wqt, "wkt": wkt, "wvt": wvt, "wut": wut, "wot": wot,
              "cb": cb, "nw": nw_t, "scb": scb_t, "bias": bias_r}
        in_maps.append(im)
    res = run_bass_kernel_spmd(nc, in_maps, core_ids=list(range(NCORES)))
    if _dump is not None:
        return res
    return np.concatenate([r["out"] for r in res.results], axis=0)


# version 20
# speedup vs baseline: 1.0066x; 1.0066x over previous
import numpy as np
import ml_dtypes
import concourse.bass as bass
import concourse.mybir as mybir
from concourse.bass_utils import run_bass_kernel_spmd

F32 = mybir.dt.float32
BF16 = mybir.dt.bfloat16
AF = mybir.ActivationFunctionType
ALU = mybir.AluOpType
AX = mybir.AxisListType

B_TOT, T, D = 16, 4096, 1024
H, DK, SEG = 16, 64, 256
NCH = T // SEG
NCORES = 8
EPS_LN = 1e-5
EPS_RMS = 1e-6
EPS_DEN = 1e-6
SCALE = 1.0 / 32.0

ENGINES = ("pe", "dve", "act", "pool", "sp")
ENG2BLK = {"pe": "tensor", "dve": "vector", "act": "scalar", "pool": "gpsimd",
           "sp": "sync"}
SELF_WAIT = True
FULL_SERIAL = False
SERIAL_ENGINES = set()
PE_WAIT_DMA = False
PE_STALL_EVERY = 0


class Prog:

    def __init__(self):
        self.streams = {e: [] for e in ENGINES}
        self.count = {e: 0 for e in ENGINES}
        self.dma_count = {"spd": 0, "gpd": 0, "spn": 0, "gpn": 0}
        self.barrier_count = {"spv": 0, "gpv": 0}
        self._sealed = False
        self.reg_w = {}
        self.reg_r = {}
        self.hwm = {e: {} for e in ENGINES}
        self.n_ops = 0

    def _wait(self, stream, semkey, val):
        if val <= self.hwm[stream].get(semkey, 0):
            return
        self.streams[stream].append(("wait", semkey, val))
        self.hwm[stream][semkey] = val

    def _edges(self, stream, reads, writes):
        deps = {}
        for r in reads:
            w = self.reg_w.get(r)
            if w is not None:
                deps[w[0]] = max(deps.get(w[0], -1), w[1])
        for w_ in writes:
            w = self.reg_w.get(w_)
            if w is not None:
                deps[w[0]] = max(deps.get(w[0], -1), w[1])
            for rr in self.reg_r.get(w_, ()):
                deps[rr[0]] = max(deps.get(rr[0], -1), rr[1])
        for semkey, val in deps.items():
            if semkey == stream:
                if stream == "pe" or not SELF_WAIT:
                    continue
                if val >= self.count[stream] + 1:
                    raise RuntimeError("self-wait on future")
            self._wait(stream, semkey, val)

    def op(self, stream, fn, reads=(), writes=(), tag=""):
        assert stream in ("pe", "dve", "act", "pool")
        if stream == "pe" and PE_WAIT_DMA:
            for dk in ("spd", "gpd"):
                if self.dma_count[dk] > 0:
                    self._wait(stream, dk, self.dma_count[dk])
        if stream == "pe" and PE_STALL_EVERY:
            self._pe_n = getattr(self, "_pe_n", 0) + 1
            if self._pe_n % PE_STALL_EVERY == 0:
                for e2 in ("dve", "act"):
                    if self.count.get(e2, 0) > 0:
                        self._wait(stream, e2, self.count[e2])
        if FULL_SERIAL or stream in SERIAL_ENGINES or SERIAL_ENGINES & set(ENGINES):
            for e2 in ENGINES:
                if e2 == stream:
                    continue
                if (FULL_SERIAL or stream in SERIAL_ENGINES or e2 in SERIAL_ENGINES) \
                        and self.count.get(e2, 0) > 0:
                    self._wait(stream, e2, self.count[e2])
            if FULL_SERIAL or stream in SERIAL_ENGINES:
                for dk in ("spd", "gpd"):
                    if self.dma_count[dk] > 0:
                        self._wait(stream, dk, self.dma_count[dk])
        self._edges(stream, reads, writes)
        self.count[stream] += 1
        cnt = self.count[stream]
        self.streams[stream].append(("op", fn, stream, 1, tag))
        for r in reads:
            self.reg_r.setdefault(r, []).append((stream, cnt))
        for w in writes:
            self.reg_w[w] = (stream, cnt)
            self.reg_r[w] = []
        self.n_ops += 1

    def dma(self, queue, out_ap, in_ap, reads=(), writes=(), tag="",
            narrow=False):
        stream = "sp" if queue == "sp" else "pool"
        if narrow:
            assert not self._sealed
            semkey = "spn" if queue == "sp" else "gpn"
            mark = None
        else:
            semkey = "spd" if queue == "sp" else "gpd"
            viskey = "spv" if queue == "sp" else "gpv"
            mark = (viskey, self.barrier_count[viskey] + 1)
        self._edges(stream, reads, writes)
        self.dma_count[semkey] += 16
        cnt = self.dma_count[semkey]

        def fn(eng, o=out_ap, i=in_ap):
            return eng.dma_start(o, i)
        self.streams[stream].append(("op", fn, semkey, 16, tag))
        if mark is None:
            mark = (semkey, cnt)
        for r in reads:
            self.reg_r.setdefault(r, []).append(mark)
        for w in writes:
            self.reg_w[w] = mark
            self.reg_r[w] = []
        self.n_ops += 1

    def barrier(self, queue):
        stream = "sp" if queue == "sp" else "pool"
        semkey = "spd" if queue == "sp" else "gpd"
        viskey = "spv" if queue == "sp" else "gpv"
        self._wait(stream, semkey, self.dma_count[semkey])

        def fn(eng):
            return eng.nop()
        self.streams[stream].append(("op", fn, viskey, 1, "barrier"))
        self.barrier_count[viskey] += 1

    def seal_narrow(self):
        self._sealed = True
        for r, w in list(self.reg_w.items()):
            if w[0] in ("spn", "gpn"):
                self.reg_w[r] = (w[0], self.dma_count[w[0]])

    def final_store_wait(self):
        self._wait("sp", "spd", self.dma_count["spd"])
        self._wait("pool", "gpd", self.dma_count["gpd"])

    def emit(self, nc):
        sems = {}
        for s in list(ENGINES) + ["spd", "gpd", "spn", "gpn", "spv", "gpv"]:
            sems[s] = nc.alloc_semaphore(f"sem_{s}")
        blk = nc.Block("main")
        with blk:
            for e in ENGINES:
                items = self.streams[e]

                def body(eng, items=items):
                    for it in items:
                        if it[0] == "wait":
                            eng.wait_ge(sems[it[1]], it[2])
                        else:
                            _, fn, semkey, inc, tag = it
                            fn(eng).then_inc(sems[semkey], inc)
                getattr(blk, ENG2BLK[e])(body)


def build(nc, n_seq, n_chunks, dump=None, skip=()):
    p = Prog()
    gen = {}

    def mm_start(key):
        if not gen.get(key, False):
            gen[key] = True
            return True
        return False

    def gen_reset(*keys):
        for k in keys:
            gen[k] = False

    x_d = nc.dram_tensor("x", [n_seq, T, D], F32, kind="ExternalInput").ap()
    out_d = nc.dram_tensor("out", [n_seq, T, D], F32, kind="ExternalOutput").ap()
    w_d = {}
    for nme in ["wqt", "wkt", "wvt", "wut", "wot"]:
        w_d[nme] = nc.dram_tensor(nme, [D, D], F32, kind="ExternalInput").ap()
    cb_d = nc.dram_tensor("cb", [D, D], F32, kind="ExternalInput").ap()
    nw_d = nc.dram_tensor("nw", [128, 8], F32, kind="ExternalInput").ap()
    scb_d = nc.dram_tensor("scb", [128, 8], F32, kind="ExternalInput").ap()
    bias_d = nc.dram_tensor("bias", [1, D], F32, kind="ExternalInput").ap()

    ident_np = np.eye(128, dtype=np.float32).astype(ml_dtypes.bfloat16)
    atm = np.triu(np.ones((128, 128), np.float32))
    atmask_np = np.concatenate([atm, np.ones((128, 128), np.float32), atm],
                               axis=1).astype(ml_dtypes.bfloat16)
    dux0_np = np.tril(np.ones((128, 128), np.float32)).astype(ml_dtypes.bfloat16)
    dux1_np = np.concatenate([np.ones((128, 128), np.float32),
                              np.tril(np.ones((128, 128), np.float32))],
                             axis=1).astype(ml_dtypes.bfloat16)
    rc_np = np.stack([1.0 / np.arange(1, 129), 1.0 / np.arange(129, 257)],
                     axis=1).astype(np.float32)
    sel_np = np.zeros((16, 16, 128), np.float32)
    for nn in range(16):
        sel_np[nn, nn, :] = 1.0
    sel_np = sel_np.astype(ml_dtypes.bfloat16)
    ones_np = np.ones((1, 512), np.float32).astype(ml_dtypes.bfloat16)

    ident_d = nc.inline_tensor(np.ascontiguousarray(ident_np), "c_ident").ap()
    atmask_d = nc.inline_tensor(atmask_np, "c_atmask").ap()
    dux0_d = nc.inline_tensor(dux0_np, "c_dux0").ap()
    dux1_d = nc.inline_tensor(dux1_np, "c_dux1").ap()
    rc_d = nc.inline_tensor(rc_np, "c_recipct").ap()
    sel_d = nc.inline_tensor(np.ascontiguousarray(sel_np.reshape(16 * 16, 128)),
                             "c_sel").ap()
    ones_d = nc.inline_tensor(ones_np, "c_ones").ap()

    A = nc.alloc_sbuf_tensor
    wsb = {nme: A(nme + "_sb", [128, 8 * D], BF16) for nme in
           ["wqt", "wkt", "wvt", "wut", "wot"]}
    cbT = A("cbT_sb", [128, 8 * D], BF16)
    arch = A("arch", [128, 8 * 960], BF16)
    xs = [A(f"xbuf{i}", [128, 2 * D], F32) for i in range(2)]
    tm_sc = A("tm_sc", [128, 2 * D], BF16)
    xln_tm = tm_sc
    xn_tm = tm_sc
    xln_fm = A("xln_fm", [128, 2 * D], BF16)
    xn_fm = A("xn_fm", [128, 2 * D], BF16)
    y_tm = A("y_tm", [128, 2 * D], BF16)
    y_fm = A("y_fm", [128, 2 * D], BF16)
    u_fm = A("u_fm", [128, 2 * D], BF16)
    pq_fm = A("pq_fm", [128, 2 * D], BF16)
    pk_fm = A("pk_fm", [128, 2 * D], BF16)
    pk_tm = A("pk_tm", [128, 2 * D], BF16)
    v_tm = A("v_tm", [128, 2 * 16 * 65], BF16)
    eluE = A("eluE", [128, 512], BF16)
    eluR = A("eluR", [128, 512], BF16)
    sexcl = A("sexcl", [128, 8 * 65], BF16)
    zpad = A("zpad", [128, 16], BF16)
    at_sb = [A(f"atsb{i}", [128, 384], BF16) for i in range(2)]
    x_out = A("x_out", [128, 2 * D], F32)
    dxm = A("dxm", [128, 256], F32)
    gsb = A("gsb", [128, 8 * 256], BF16)
    bcsb = A("bcsb", [128, 2 * 256], BF16)
    gfm = A("gfm", [16, 256], BF16)
    gates = [A(f"gates{i}", [128, 17], F32) for i in range(2)]
    logits = [A(f"logits{i}", [128, 17], F32) for i in range(2)]
    gates_bf = A("gates_bf", [128, 16], BF16)
    mstk = A("mstk", [128, 8 * 16], BF16)
    msum = A("msum", [128, 8], F32)
    ST_NAMES = ["s1", "s2a", "s2b", "s2c", "m", "var", "u1", "rstd1", "sc2",
                "b1", "b2", "q2", "rr", "mx", "negb", "z1", "rz", "scr", "lnu",
                "lnq", "rcp", "tmp"]
    st = {nme: A("st_" + nme, [128, 4], F32) for nme in ST_NAMES}
    nmr_bf = A("nmr_bf", [128, 2], BF16)
    rr_bf = A("rr_bf", [128, 2], BF16)
    rr_row = A("rr_row", [1, 256], BF16)
    nmr_row = A("nmr_row", [1, 256], BF16)
    den_sb = A("den_sb", [128, 32], F32)
    rden = A("rden", [128, 32], F32)
    ident = A("identsb", [128, 128], BF16)
    atmask = A("atmasksb", [128, 384], BF16)
    dux0 = A("dux0sb", [128, 128], BF16)
    dux1 = A("dux1sb", [128, 256], BF16)
    rcct = A("rcctsb", [128, 2], F32)
    sel = A("selsb", [16, 16 * 128], BF16)
    ones_bf = A("onessb", [1, 512], BF16)
    bias_row = A("bias_row", [1, D], BF16)
    nw_sb = A("nw_sb", [128, 8], F32)
    scb_sb = A("scb_sb", [128, 8], F32)
    wstage = [A("wstage0", [128, D], F32)[:, :], A("wstage1", [128, D], F32)[:, :]]
    cbrow = A("cbrow", [128, D], BF16)
    dbg_d = None
    if dump is not None:
        dbg_d = nc.dram_tensor("dbg", [128, 2 * D], F32,
                               kind="ExternalOutput").ap()

    P = nc.alloc_psum_tensor
    s_psA = P("s_psA", [128, 4 * 65], F32)
    s_psB = P("s_psB", [128, 4 * 65], F32)
    pa = P("pa", [128, 512], F32)
    pb = P("pb", [128, 512], F32)
    tr = P("tr", [128, 512], BF16)
    at_ps = P("at_ps", [128, 384], F32)
    ybank = P("ybank", [128, 1024], F32)

    def emit_mm(psum_region, out_ap, lhsT, rhs, reads, key=None, tp=None,
                force_start=None, tag=""):
        k = key or psum_region
        strt = mm_start(k) if force_start is None else force_start

        def fn(eng, o=out_ap, l=lhsT, r=rhs, s=strt, t=tp):
            return eng.matmul(o, l, r, start=s, stop=True, tile_position=t)
        p.op("pe", fn, reads=reads, writes=(psum_region,), tag=tag)

    def transpose(psum_ap, in_ap, reads, writes, tag=""):
        def fn(eng, o=psum_ap, i=in_ap):
            return eng.transpose(o, i, ident[:, :])
        p.op("pe", fn, reads=tuple(reads) + ("ident",), writes=writes, tag=tag)

    def act(out_ap, in_ap, func, reads, writes, scale=1.0, bias=0.0,
            accum=None, tag=""):
        def fn(eng, o=out_ap, i=in_ap, f=func, s=scale, b=bias, a=accum):
            return eng.activation(o, i, f, bias=b, scale=s, accum_out=a)
        p.op("act", fn, reads=reads, writes=writes, tag=tag)

    def act_copy(out_ap, in_ap, reads, writes, tag=""):
        def fn(eng, o=out_ap, i=in_ap):
            return eng.copy(o, i)
        p.op("act", fn, reads=reads, writes=writes, tag=tag)

    def dve(method, out_ap, *args, reads=(), writes=(), tag=""):
        def fn(eng, m=method, o=out_ap, a=args):
            return getattr(eng, m)(o, *a)
        p.op("dve", fn, reads=reads, writes=writes, tag=tag)

    def pool_op(method, out_ap, *args, reads=(), writes=(), tag=""):
        def fn(eng, m=method, o=out_ap, a=args):
            return getattr(eng, m)(o, *a)
        p.op("pool", fn, reads=reads, writes=writes, tag=tag)

    def stt(stream, out_ap, in0, scal, in1, o0, o1, reads, writes, tag=""):
        def fn(eng, o=out_ap, a=in0, s=scal, b=in1, x=o0, y=o1):
            return eng.scalar_tensor_tensor(o, a, s, b, x, y)
        p.op(stream, fn, reads=reads, writes=writes, tag=tag)

    p.dma("sp", ident[:, :], ident_d, writes=("ident",), narrow=True)
    p.dma("sp", atmask[:, :], atmask_d, writes=("atmask",), narrow=True)
    p.dma("sp", dux0[:, :], dux0_d, writes=("dux0",), narrow=True)
    p.dma("sp", dux1[:, :], dux1_d, writes=("dux1",), narrow=True)
    p.dma("sp", rcct[:, :], rc_d, writes=("rcct",), narrow=True)
    for _kt in range(16):
        p.dma("sp", sel[:, _kt * 128:(_kt + 1) * 128],
              sel_d[_kt * 16:(_kt + 1) * 16, :], writes=("sel",), narrow=True)
    p.dma("sp", ones_bf[:, :], ones_d, writes=("ones",), narrow=True)
    p.dma("sp", nw_sb[:, :], nw_d, writes=("nw",), narrow=True)
    p.dma("sp", scb_sb[:, :], scb_d, writes=("scb",), narrow=True)
    p.dma("pool", bias_row[:, :], bias_d, writes=("bias_row",), narrow=True)
    dve("memset", v_tm[:, :], 1.0, writes=("v_tm",))
    dve("memset", arch[:, :], 0.0, writes=("arch",))
    dve("memset", zpad[:, :], 0.0, writes=("zpad",))

    def load_x(b, c, buf):
        for i in range(2):
            t0 = c * SEG + i * 128
            p.dma("sp", xs[buf][:, i * D:(i + 1) * D], x_d[b, t0:t0 + 128, :],
                  reads=(f"xb{buf}_{i}",), writes=(f"xb{buf}_{i}",),
                  tag=f"ldx{b}.{c}")

    load_x(0, 0, 0)

    for wi, nme in enumerate(["wqt", "wkt", "wvt"]):
        for kt in range(8):
            sb = (wi * 8 + kt) % 2
            p.dma("sp", wstage[sb], w_d[nme][kt * 128:(kt + 1) * 128, :],
                  reads=(f"wstage{sb}",), writes=(f"wstage{sb}",))
            if sb == 1:
                p.barrier("sp")
            dve("tensor_scalar_mul", wsb[nme][:, kt * D:(kt + 1) * D],
                wstage[sb], nw_sb[:, kt:kt + 1],
                reads=(f"wstage{sb}", "nw"), writes=(f"{nme}_k{kt}",))
    for nme in ["wut", "wot"]:
        for kt in range(8):
            p.dma("pool", wsb[nme][:, kt * D:(kt + 1) * D],
                  w_d[nme][kt * 128:(kt + 1) * 128, :],
                  writes=(f"{nme}_k{kt}",))
    p.barrier("pool")
    for nt in range(8):
        sb = nt % 2
        p.dma("sp", wstage[sb], cb_d[nt * 128:(nt + 1) * 128, :],
              reads=(f"wstage{sb}",), writes=(f"wstage{sb}",))
        p.barrier("sp")
        def cbs_fn(eng, o=cbrow[:, :], i=wstage[sb],
                   s1=scb_sb[:, nt:nt + 1]):
            return eng.tensor_scalar(o, i, s1, 1.0 / 127.0, ALU.mult, ALU.mult)
        p.op("dve", cbs_fn, reads=(f"wstage{sb}", "scb"), writes=("cbrow",))
        for k2 in (0, 4):
            for kk in range(4):
                kt = k2 + kk
                transpose(tr[:, kk * 128:(kk + 1) * 128],
                          cbrow[:, kt * 128:(kt + 1) * 128],
                          reads=("cbrow",), writes=("tr0", "tr1"))
            dst = cbT[:, :].rearrange("p (k g) -> p k g", k=8)[
                :, k2:k2 + 4, nt * 128:(nt + 1) * 128]
            src = tr[:, :].rearrange("p (k g) -> p k g", k=4)
            act_copy(dst, src, reads=("tr0", "tr1"), writes=("cbT",))

    def ln_stats2(x_aps, xregs, stc, eps, rms, sq_target, sq_region, pieces):
        c2 = (slice(None), slice(stc, stc + 2))
        accs = ["s2a", "s2b", "s2c"]
        for i in range(2):
            dve("tensor_reduce", st["s1"][:, stc + i:stc + i + 1], x_aps[i],
                AX.X, ALU.add, reads=(xregs[i],), writes=("st_s1",))
            off = 0
            for pi, plen in enumerate(pieces):
                act(sq_target[:, 0:plen], x_aps[i][:, off:off + plen],
                    AF.Square, reads=(xregs[i],), writes=(sq_region,),
                    accum=st[accs[pi]][:, stc + i:stc + i + 1])
                p.reg_w["st_" + accs[pi]] = p.reg_w[sq_region]
                p.reg_r["st_" + accs[pi]] = []
                off += plen
        dve("tensor_scalar_mul", st["m"][c2], st["s1"][c2], 1.0 / 1024.0,
            reads=("st_s1",), writes=("st_m",))
        dve("tensor_add", st["tmp"][c2], st["s2a"][c2], st["s2b"][c2],
            reads=("st_s2a", "st_s2b"), writes=("st_tmp",))
        if len(pieces) > 2:
            dve("tensor_add", st["tmp"][c2], st["tmp"][c2], st["s2c"][c2],
                reads=("st_tmp", "st_s2c"), writes=("st_tmp",))
        dve("tensor_mul", st["q2"][c2], st["m"][c2], st["m"][c2],
            reads=("st_m",), writes=("st_q2",))
        stt("dve", st["var"][c2], st["tmp"][c2], 1.0 / 1024.0, st["q2"][c2],
            ALU.mult, ALU.subtract, ("st_tmp", "st_q2"), ("st_var",))
        dve("tensor_scalar_add", st["u1"][c2], st["var"][c2], float(eps),
            reads=("st_var",), writes=("st_u1",))
        act(st["lnu"][c2], st["u1"][c2], AF.Ln,
            reads=("st_u1",), writes=("st_lnu",))
        act(st["rstd1"][c2], st["lnu"][c2], AF.Exp, scale=-0.5,
            reads=("st_lnu",), writes=("st_rstd1",))
        stt("dve", st["b1"][c2], st["m"][c2], -1.0, st["rstd1"][c2],
            ALU.mult, ALU.mult, ("st_m", "st_rstd1"), ("st_b1",))
        if rms:
            dve("tensor_mul", st["rcp"][c2], st["rstd1"][c2], st["rstd1"][c2],
                reads=("st_rstd1",), writes=("st_rcp",))
            def q2f(eng, o=st["q2"][c2], x=st["rcp"][c2]):
                return eng.tensor_scalar(o, x, -float(eps), 1.0 + EPS_RMS,
                                         ALU.mult, ALU.add)
            p.op("dve", q2f, reads=("st_rcp",), writes=("st_q2",))
            act(st["lnq"][c2], st["q2"][c2], AF.Ln, reads=("st_q2",),
                writes=("st_lnq",))
            act(st["rr"][c2], st["lnq"][c2], AF.Exp, scale=-0.5,
                reads=("st_lnq",), writes=("st_rr",))
            dve("tensor_mul", st["sc2"][c2], st["rstd1"][c2], st["rr"][c2],
                reads=("st_rstd1", "st_rr"), writes=("st_sc2",))
            stt("dve", st["b2"][c2], st["m"][c2], -1.0, st["sc2"][c2],
                ALU.mult, ALU.mult, ("st_m", "st_sc2"), ("st_b2",))

    def transpose_set(src, src_region, dst, dst_region, direction,
                      accum_ms=False):
        if direction == "tm2fm":
            tiles = [(i * 1024 + j * 128, j * 256 + i * 128)
                     for j in range(8) for i in range(2)]
        else:
            tiles = [(g * 256 + i * 128, i * 1024 + g * 128)
                     for i in range(2) for g in range(8)]
        for half in range(4):
            for kk in range(4):
                s_off, _ = tiles[half * 4 + kk]
                transpose(tr[:, kk * 128:(kk + 1) * 128],
                          src[:, s_off:s_off + 128],
                          reads=(src_region,), writes=("tr0", "tr1"))
            if accum_ms:
                for jj in range(2):
                    j = half * 2 + jj
                    d_off = tiles[half * 4 + jj * 2][1]
                    act(dst[:, d_off:d_off + 256],
                        tr[:, jj * 256:(jj + 1) * 256], AF.Copy,
                        reads=("tr0", "tr1"), writes=(dst_region,),
                        accum=msum[:, j:j + 1])
                    p.reg_w[f"msum_{j}"] = p.reg_w[dst_region]
                    p.reg_r[f"msum_{j}"] = []
            else:
                d_off = tiles[half * 4][1]
                if half % 2 == 0:
                    act_copy(dst[:, d_off:d_off + 512], tr[:, 0:512],
                             reads=("tr0", "tr1"), writes=(dst_region,))
                else:
                    dve("tensor_copy", dst[:, d_off:d_off + 512], tr[:, 0:512],
                        reads=("tr0", "tr1"), writes=(dst_region,))
            yield

    def proj_fm(wname, epilogue, src_fm, src_region):
        for w in range(4):
            ps, psr = (pa, "pa") if w % 2 == 0 else (pb, "pb")
            gen_reset(psr)
            for half in range(2):
                ot = 2 * w + half
                for kt in range(8):
                    emit_mm(psr, ps[:, half * 256:(half + 1) * 256],
                            wsb[wname][:, kt * D + ot * 128:kt * D + (ot + 1) * 128],
                            src_fm[:, kt * 256:(kt + 1) * 256],
                            reads=(f"{wname}_k{kt}", src_region), key=psr)
            epilogue(w, ps, psr)
            yield

    def phase_A1(b, c):
        buf = c % 2
        xreg = (f"xb{buf}_0", f"xb{buf}_1")
        stc = (c % 2) * 2
        if c == 0:
            dve("memset", gfm[:, :], 0.0, writes=("gfm",))
            gen_reset("s_psA0", "s_psA64", "s_psB0", "s_psB64")
        p.barrier("sp")
        if c + 1 < n_chunks:
            load_x(b, c + 1, (c + 1) % 2)
        elif b + 1 < n_seq:
            load_x(b + 1, 0, (c + 1) % 2)
        x_aps = [xs[buf][:, i * D:(i + 1) * D] for i in range(2)]
        ln_stats2(x_aps, xreg, stc, EPS_LN, True, at_ps, "at_ps",
                  (384, 384, 256))
        yield
        for i in range(2):
            def aff(eng, o=xln_tm[:, i * D:(i + 1) * D], x=x_aps[i],
                    s1=st["rstd1"][:, stc + i:stc + i + 1],
                    s2=st["b1"][:, stc + i:stc + i + 1]):
                return eng.tensor_scalar(o, x, s1, s2, ALU.mult, ALU.add)
            p.op("dve", aff, reads=(xreg[i], "st_rstd1", "st_b1"),
                 writes=("tm_sc",))
            act_copy(nmr_bf[:, i:i + 1], st["b1"][:, stc + i:stc + i + 1],
                     reads=("st_b1",), writes=("nmr_bf",))
            yield

    def phase_A2(b, c):
        buf = c % 2
        xreg = (f"xb{buf}_0", f"xb{buf}_1")
        stc = (c % 2) * 2
        yield from transpose_set(xln_tm, "tm_sc", xln_fm, "xln_fm", "tm2fm",
                                 accum_ms=(c < n_chunks - 1))
        if c < n_chunks - 1:
            for j in range(8):
                dve("tensor_scalar_mul", mstk[:, j * 16 + c:j * 16 + c + 1],
                    msum[:, j:j + 1], 1.0 / 256.0,
                    reads=(f"msum_{j}",), writes=("mstk",))
        act_copy(rr_bf[:, 0:2], st["rr"][:, stc:stc + 2], reads=("st_rr",),
                 writes=("rr_bf",))
        for i in range(2):
            transpose(tr[0:1, i * 128:(i + 1) * 128], nmr_bf[:, i:i + 1],
                      reads=("nmr_bf",), writes=("tr0", "tr1"))
            act_copy(nmr_row[0:1, i * 128:(i + 1) * 128],
                     tr[0:1, i * 128:(i + 1) * 128],
                     reads=("tr0", "tr1"), writes=("nmr_row",))
            transpose(tr[0:1, 256 + i * 128:256 + (i + 1) * 128],
                      rr_bf[:, i:i + 1], reads=("rr_bf",),
                      writes=("tr0", "tr1"))
            act_copy(rr_row[0:1, i * 128:(i + 1) * 128],
                     tr[0:1, 256 + i * 128:256 + (i + 1) * 128],
                     reads=("tr0", "tr1"), writes=("rr_row",))
        gen_reset("pa")
        emit_mm("pa", pa[:, 0:256], ones_bf[0:1, 0:128], rr_row[0:1, 0:256],
                reads=("ones", "rr_row"), key="pa")
        xl3 = xln_fm[:, :].rearrange("p (g t) -> p g t", g=8)
        bc3 = pa[:, 0:256].unsqueeze(1).broadcast_to([128, 8, 256])
        dve("tensor_mul", xn_fm[:, :].rearrange("p (g t) -> p g t", g=8),
            xl3, bc3, reads=("xln_fm", "pa"), writes=("xn_fm",))
        yield

    def phase_B_gen(b, c):
        yield from _phase_B(b, c)

    def phase_B(b, c):
        for _ in _phase_B(b, c):
            pass

    def _phase_B(b, c):
        stc = (c % 2) * 2

        def u_epi(w, ps, psr):
            act_copy(u_fm[:, w * 512:(w + 1) * 512], ps[:, :], reads=(psr,),
                     writes=("u_fm",))

        def elu_epi_for(dst, dstr):
            def elu_epi(w, ps, psr):
                act(eluE[:, :], ps[:, :], AF.Exp, reads=(psr,),
                    writes=("eluE",))
                act(eluR[:, :], ps[:, :], AF.Relu, reads=(psr,),
                    writes=("eluR",))
                stt("dve", dst[:, w * 512:(w + 1) * 512], eluE[:, :], 1.0,
                    eluR[:, :], ALU.min, ALU.add, ("eluE", "eluR"), (dstr,))
            return elu_epi

        yield from proj_fm("wut", u_epi, xln_fm, "xln_fm")
        yield from proj_fm("wqt", elu_epi_for(pq_fm, "pq_fm"), xn_fm, "xn_fm")
        yield from proj_fm("wkt", elu_epi_for(pk_fm, "pk_fm"), xn_fm, "xn_fm")
        for i in range(2):
            for half in range(2):
                ps, psr = (pa, "pa") if half == 0 else (pb, "pb")
                gen_reset(psr)
                for kt in range(8):
                    emit_mm(psr, ps[:, :],
                            xn_fm[:, kt * 256 + i * 128:kt * 256 + (i + 1) * 128],
                            wsb["wvt"][:, kt * D + half * 512:kt * D + (half + 1) * 512],
                            reads=(f"wvt_k{kt}", "xn_fm"), key=psr)
                dst = v_tm[:, :].rearrange("p (ii h vv) -> p ii h vv",
                                           ii=2, h=16)[:, i,
                                                       half * 8:(half + 1) * 8,
                                                       0:64]
                src = ps[:, :].rearrange("p (h vv) -> p h vv", h=8)
                act_copy(dst, src, reads=(psr,), writes=("v_tm",))
                yield
        for i in range(2):
            ps, psr = (pa, "pa") if i == 0 else (pb, "pb")
            gen_reset(psr)
            ncols = 128 if i == 0 else 256
            for kt in range(8):
                emit_mm(psr, ps[:, 0:ncols],
                        u_fm[:, kt * 256 + i * 128:kt * 256 + (i + 1) * 128],
                        xln_fm[:, kt * 256:kt * 256 + ncols],
                        reads=("u_fm", "xln_fm"), key=psr)
            mask, mr = (dux0, "dux0") if i == 0 else (dux1, "dux1")
            dve("tensor_mul", dxm[:, 0:ncols], ps[:, 0:ncols],
                mask[:, 0:ncols], reads=(psr, mr), writes=("dxm",))
            dve("tensor_reduce", st["mx"][:, stc + i:stc + i + 1],
                dxm[:, 0:ncols], AX.X, ALU.add, reads=("dxm",),
                writes=("st_mx",))
            dve("tensor_scalar_mul", logits[i][:, c:c + 1],
                st["mx"][:, stc + i:stc + i + 1], rcct[:, i:i + 1],
                reads=("st_mx", "rcct"), writes=(f"logits{i}",))
            yield
        if c > 0:
            for i in range(2):
                ps, psr = (pa, "pa") if i == 0 else (pb, "pb")
                gen_reset(psr)
                for kt in range(8):
                    emit_mm(psr, ps[:, 0:c],
                            u_fm[:, kt * 256 + i * 128:kt * 256 + (i + 1) * 128],
                            mstk[:, kt * 16:kt * 16 + c],
                            reads=("u_fm", "mstk"), key=psr)
                act_copy(logits[i][:, 0:c], ps[:, 0:c], reads=(psr,),
                         writes=(f"logits{i}",))
        for i in range(2):
            lr, gr = f"logits{i}", f"gates{i}"
            ci = (slice(None), slice(stc + i, stc + i + 1))
            dve("tensor_reduce", st["mx"][ci], logits[i][:, 0:c + 1],
                AX.X, ALU.max, reads=(lr,), writes=("st_mx",))
            dve("tensor_scalar_mul", st["negb"][ci], st["mx"][ci], -SCALE,
                reads=("st_mx",), writes=("st_negb",))
            act(gates[i][:, 0:c + 1], logits[i][:, 0:c + 1], AF.Exp,
                scale=SCALE, bias=st["negb"][ci],
                reads=(lr, "st_negb"), writes=(gr,))
            dve("tensor_reduce", st["z1"][ci], gates[i][:, 0:c + 1],
                AX.X, ALU.add, reads=(gr,), writes=("st_z1",))
            dve("reciprocal", st["rz"][ci], st["z1"][ci],
                reads=("st_z1",), writes=("st_rz",))
            dve("tensor_scalar_mul", gates[i][:, 0:c + 1],
                gates[i][:, 0:c + 1], st["rz"][ci],
                reads=(gr, "st_rz"), writes=(gr,))

    def phase_C(b, c):
        if c > 0:
            act_copy(sexcl[:, 0:260], s_psA[:, :], reads=("s_psA",),
                     writes=("sexcl",))
            act_copy(sexcl[:, 260:520], s_psB[:, :], reads=("s_psB",),
                     writes=("sexcl",))
            for (sps, spr, h0) in ((s_psA, "s_psA", 0), (s_psB, "s_psB", 8)):
                for rw in (0, 64):
                    src_ap = sps[rw:rw + 64, :].rearrange(
                        "p (g q) -> p g q", g=4)[:, :, 64:65]
                    dst_ap = zpad[rw:rw + 64, h0 + rw // 64:h0 + 8:2].unsqueeze(2)
                    act_copy(dst_ap, src_ap, reads=(spr,), writes=("zpad",))
        yield
        last = c >= n_chunks - 1
        for gp in range(4):
            for kk, (g, i) in enumerate([(2 * gp, 0), (2 * gp, 1),
                                         (2 * gp + 1, 0), (2 * gp + 1, 1)]):
                transpose(tr[:, kk * 128:(kk + 1) * 128],
                          pk_fm[:, g * 256 + i * 128:g * 256 + (i + 1) * 128],
                          reads=("pk_fm",), writes=("tr0", "tr1"))
            act_copy(pk_tm[:, 2 * gp * 128:2 * gp * 128 + 256
                           ].rearrange("p (k f) -> p k f", k=2),
                     tr[:, :].rearrange("p (k f) -> p k f", k=4)[:, 0::2, :],
                     reads=("tr0", "tr1"), writes=("pk_tm",))
            dve("tensor_copy",
                pk_tm[:, 1024 + 2 * gp * 128:1024 + 2 * gp * 128 + 256
                      ].rearrange("p (k f) -> p k f", k=2),
                tr[:, :].rearrange("p (k f) -> p k f", k=4)[:, 1::2, :],
                reads=("tr0", "tr1"), writes=("pk_tm",))
            yield
            if not last:
                for h in range(4 * gp, 4 * gp + 4):
                    hp = h // 2
                    sps, spr = (s_psA, "s_psA") if hp < 4 else (s_psB, "s_psB")
                    col = (hp % 4) * 65
                    rw = (h % 2) * 64
                    for i in range(2):
                        emit_mm(spr, sps[rw:rw + 64, col:col + 65],
                                pk_tm[:, i * D + h * 64:i * D + (h + 1) * 64],
                                v_tm[:, i * 1040 + h * 65:i * 1040 + (h + 1) * 65],
                                reads=("pk_tm", "v_tm"), key=f"{spr}{rw}",
                                tp=(0, rw))
                    if h % 2 == 1:
                        yield
        if not last:
            for (sps, spr, hp0) in ((s_psA, "s_psA", 0), (s_psB, "s_psB", 4)):
                src_ap = sps[:, :].rearrange("p (g q) -> p g q", g=4)[:, :, 0:64]
                dst_ap = arch[:, :].rearrange("p (g q) -> p g q", g=8)[
                    :, hp0:hp0 + 4, c * 64:(c + 1) * 64]
                act_copy(dst_ap, src_ap, reads=(spr,), writes=("arch",))
            yield

    def phase_D(b, c):
        gen_reset("den")

        def emit_m4(h):
            hp = h // 2
            rw = (h % 2) * 64
            aps, apr = (at_ps, "at_ps") if h % 2 == 0 else (pb, "pb")
            gen_reset(apr)
            emit_mm(apr, aps[:, 0:256],
                    pk_fm[rw:rw + 64, hp * 256:hp * 256 + 128],
                    pq_fm[rw:rw + 64, hp * 256:(hp + 1) * 256],
                    reads=("pk_fm", "pq_fm"), key=apr, tp=(rw, 0))
            emit_mm(apr, aps[:, 256:384],
                    pk_fm[rw:rw + 64, hp * 256 + 128:(hp + 1) * 256],
                    pq_fm[rw:rw + 64, hp * 256 + 128:(hp + 1) * 256],
                    reads=("pk_fm", "pq_fm"), key=apr, tp=(rw, 0))

        emit_m4(0)
        for hhalf in range(2):
            gen_reset("ybA", "ybB")
            for h8 in range(8):
                h = hhalf * 8 + h8
                if h < 15:
                    emit_m4(h + 1)
                hp = h // 2
                rw = (h % 2) * 64
                slot = at_sb[h % 2]
                slotr = f"atsb{h % 2}"
                aps, apr = (at_ps, "at_ps") if h % 2 == 0 else (pb, "pb")
                dve("tensor_mul", slot[:, 0:128], aps[:, 0:128],
                    atmask[:, 0:128], reads=(apr, "atmask"), writes=(slotr,))
                dve("tensor_mul", slot[:, 128:384], aps[:, 128:384],
                    atmask[:, 128:384], reads=(apr, "atmask"),
                    writes=(slotr + "x",))
                par = h8 % 2
                ybr = "ybA" if par == 0 else "ybB"
                ktiles = {0: [(0, slice(0, 128))],
                          1: [(0, slice(128, 256)), (1, slice(256, 384))]}
                for i in range(2):
                    ycol = par * 512 + i * 256 + (h8 // 2) * 64
                    for (dt, sl) in ktiles[i]:
                        slr = slotr if sl.start == 0 else slotr + "x"
                        emit_mm("ybank", ybank[:, ycol:ycol + 64],
                                slot[:, sl],
                                v_tm[:, dt * 1040 + h * 65:dt * 1040 + h * 65 + 64],
                                reads=(slr, "v_tm"), key=ybr)
                        emit_mm("pa", pa[:, i * 16 + h:i * 16 + h + 1],
                                slot[:, sl],
                                v_tm[:, dt * 1040 + h * 65 + 64:dt * 1040 + h * 65 + 65],
                                reads=(slr, "v_tm"), key="den")
                    if c > 0:
                        sbase = (hp % 4) * 65 + 260 * (hp // 4)
                        emit_mm("ybank", ybank[:, ycol:ycol + 64],
                                pq_fm[rw:rw + 64, hp * 256 + i * 128:hp * 256 + (i + 1) * 128],
                                sexcl[rw:rw + 64, sbase:sbase + 64],
                                reads=("pq_fm", "sexcl"), key=ybr,
                                tp=(rw, 0))
                        emit_mm("pa", pa[:, i * 16 + h:i * 16 + h + 1],
                                pq_fm[:, hp * 256 + i * 128:hp * 256 + (i + 1) * 128],
                                zpad[:, h:h + 1],
                                reads=("pq_fm", "zpad"), key="den")
                yield
            for i in range(2):
                cl = i * 16 + hhalf * 8
                act(den_sb[:, cl:cl + 8], pa[:, cl:cl + 8], AF.Copy,
                    bias=EPS_DEN, reads=("pa",), writes=("den_sb",))
                dve("reciprocal", rden[:, cl:cl + 8], den_sb[:, cl:cl + 8],
                    reads=("den_sb",), writes=("rden",))
                for par in range(2):
                    y3 = ybank[:, par * 512 + i * 256:par * 512 + (i + 1) * 256
                               ].rearrange("p (h vv) -> p h vv", h=4)
                    rv = rden[:, cl + par:cl + 8:2].unsqueeze(2).broadcast_to(
                        [128, 4, 64])
                    dst = y_tm[:, i * D + hhalf * 512:i * D + (hhalf + 1) * 512
                               ].rearrange("p (h vv) -> p h vv", h=8)[:, par::2, :]
                    dve("tensor_mul", dst, y3, rv,
                        reads=("ybank", "rden"), writes=("y_tm",))
            tiles = [(i * 1024 + j * 128, j * 256 + i * 128)
                     for j in range(hhalf * 4, hhalf * 4 + 4)
                     for i in range(2)]
            for half in range(2):
                for kk in range(4):
                    s_off, _ = tiles[half * 4 + kk]
                    transpose(tr[:, kk * 128:(kk + 1) * 128],
                              y_tm[:, s_off:s_off + 128],
                              reads=("y_tm",), writes=("tr0", "tr1"))
                d_off = tiles[half * 4][1]
                if half == 0:
                    act_copy(y_fm[:, d_off:d_off + 512], tr[:, 0:512],
                             reads=("tr0", "tr1"), writes=("y_fm",))
                else:
                    dve("tensor_copy", y_fm[:, d_off:d_off + 512],
                        tr[:, 0:512], reads=("tr0", "tr1"), writes=("y_fm",))
                yield

    def phase_E(b, c):
        buf = c % 2
        xreg = (f"xb{buf}_0", f"xb{buf}_1")
        stc = (c % 2) * 2
        for i in range(2):
            for half in range(2):
                ps, psr = (pa, "pa") if half == 0 else (pb, "pb")
                gen_reset(psr)
                for kt in range(8):
                    emit_mm(psr, ps[:, :],
                            y_fm[:, kt * 256 + i * 128:kt * 256 + (i + 1) * 128],
                            wsb["wot"][:, kt * D + half * 512:kt * D + (half + 1) * 512],
                            reads=(f"wot_k{kt}", "y_fm"), key=psr)
                emit_mm(psr, ps[:, :],
                        nmr_row[0:1, i * 128:(i + 1) * 128],
                        ones_bf[0:1, 0:512],
                        reads=("nmr_row", "ones"), key=psr)
                stt("dve", x_out[:, i * D + half * 512:i * D + (half + 1) * 512],
                    xs[buf][:, i * D + half * 512:i * D + (half + 1) * 512],
                    st["rstd1"][:, stc + i:stc + i + 1], ps[:, :],
                    ALU.mult, ALU.add,
                    (xreg[i], "st_rstd1", psr), ("x_out",))
                yield

    def phase_F(b, c):
        buf = c % 2
        xreg = (f"xb{buf}_0", f"xb{buf}_1")
        if c > 0:
            for i in range(2):
                act_copy(gates_bf[:, 0:c], gates[i][:, 0:c],
                         reads=(f"gates{i}",), writes=("gates_bf",))
                transpose(tr[0:c, i * 128:(i + 1) * 128], gates_bf[:, 0:c],
                          reads=("gates_bf",), writes=("tr0",))
                act_copy(gfm[0:c, i * 128:(i + 1) * 128],
                         tr[0:c, i * 128:(i + 1) * 128],
                         reads=("tr0",), writes=("gfm",))
            yield
            for hhalf in range(2):
                gen_reset("ybA", "ybB")

                def emit_bc_gtt(nn):
                    bslot = nn % 2
                    gen_reset("pa")
                    emit_mm("pa", pa[:, 0:256],
                            sel[:, nn * 128:(nn + 1) * 128], gfm[:, :],
                            reads=("sel", "gfm"), key="pa")
                    act_copy(bcsb[:, bslot * 256:(bslot + 1) * 256],
                             pa[:, 0:256], reads=("pa",),
                             writes=(f"bcsb{bslot}",))
                    for hp4 in range(4):
                        hp = hhalf * 4 + hp4
                        gslot = bslot * 4 + hp4
                        dve("tensor_mul",
                            gsb[:, gslot * 256:(gslot + 1) * 256],
                            pq_fm[:, hp * 256:(hp + 1) * 256],
                            bcsb[:, bslot * 256:(bslot + 1) * 256],
                            reads=("pq_fm", f"bcsb{bslot}"),
                            writes=(f"gsb{gslot}",))

                emit_bc_gtt(0)
                for nn in range(c):
                    if nn + 1 < c:
                        emit_bc_gtt(nn + 1)
                    bslot = nn % 2
                    for i in range(2):
                        for h8 in range(8):
                            h = hhalf * 8 + h8
                            hp4 = h8 // 2
                            hp = hhalf * 4 + hp4
                            gslot = bslot * 4 + hp4
                            rw = (h % 2) * 64
                            par = h8 % 2
                            ybr = "ybA" if par == 0 else "ybB"
                            ycol = par * 512 + i * 256 + (h8 // 2) * 64
                            emit_mm("ybank",
                                    ybank[:, ycol:ycol + 64],
                                    gsb[rw:rw + 64,
                                        gslot * 256 + i * 128:gslot * 256 + (i + 1) * 128],
                                    arch[rw:rw + 64,
                                         hp * 960 + nn * 64:hp * 960 + (nn + 1) * 64],
                                    reads=(f"gsb{gslot}", "arch"), key=ybr,
                                    tp=(rw, 0))
                    yield
                for i in range(2):
                    for par in range(2):
                        xo3 = x_out[:, i * D + hhalf * 512:i * D + (hhalf + 1) * 512
                                    ].rearrange("p (h vv) -> p h vv", h=8)[:, par::2, :]
                        yb3 = ybank[:, par * 512 + i * 256:par * 512 + (i + 1) * 256
                                    ].rearrange("p (h vv) -> p h vv", h=4)
                        stt("dve", xo3, xo3, gates[i][:, c:c + 1], yb3,
                            ALU.mult, ALU.add,
                            ("x_out", f"gates{i}", "ybank"), ("x_out",))
        else:
            for i in range(2):
                dve("tensor_scalar_mul", x_out[:, i * D:(i + 1) * D],
                    x_out[:, i * D:(i + 1) * D], gates[i][:, c:c + 1],
                    reads=("x_out", f"gates{i}"), writes=("x_out",))
        for i in range(2):
            pool_op("tensor_add", x_out[:, i * D:(i + 1) * D],
                    x_out[:, i * D:(i + 1) * D],
                    xs[buf][:, i * D:(i + 1) * D],
                    reads=("x_out", xreg[i]), writes=("x_out",))
        yield

    def phase_G2(b, c):
        stc = (c % 2) * 2
        ln_stats2([x_out[:, i * D:(i + 1) * D] for i in range(2)],
                  ("x_out", "x_out"), stc, EPS_LN, False, ybank, "ybank",
                  (512, 512))
        for i in range(2):
            def aff3(eng, o=y_tm[:, i * D:(i + 1) * D],
                     x=x_out[:, i * D:(i + 1) * D],
                     s1=st["rstd1"][:, stc + i:stc + i + 1],
                     s2=st["b1"][:, stc + i:stc + i + 1]):
                return eng.tensor_scalar(o, x, s1, s2, ALU.mult, ALU.add)
            p.op("dve", aff3, reads=("x_out", "st_rstd1", "st_b1"),
                 writes=("y_tm",))
        yield
        yield from transpose_set(y_tm, "y_tm", y_fm, "y_fm", "tm2fm")
        for i in range(2):
            for half in range(2):
                ps = ybank[:, half * 512:(half + 1) * 512]
                key = "ybA" if half == 0 else "ybB"
                gen_reset(key)
                for kt in range(8):
                    emit_mm("ybank", ps,
                            y_fm[:, kt * 256 + i * 128:kt * 256 + (i + 1) * 128],
                            cbT[:, kt * D + half * 512:kt * D + (half + 1) * 512],
                            reads=("cbT", "y_fm"), key=key)
                emit_mm("ybank", ps, ones_bf[0:1, 0:128],
                        bias_row[0:1, half * 512:(half + 1) * 512],
                        reads=("ones", "bias_row"), key=key)
                dve("tensor_add",
                    x_out[:, i * D + half * 512:i * D + (half + 1) * 512],
                    x_out[:, i * D + half * 512:i * D + (half + 1) * 512],
                    ps, reads=("x_out", "ybank"), writes=("x_out",))
                yield
        if dump is not None and dump[1] == (b, c):
            nm = dump[0]
            tens = {"xln_fm": xln_fm, "xn_fm": xn_fm, "u_fm": u_fm,
                    "pq_fm": pq_fm, "pk_fm": pk_fm, "pk_tm": pk_tm,
                    "v_tm": v_tm, "y_tm": y_tm, "y_fm": y_fm,
                    "x_out": x_out, "bs": x_out, "arch": arch,
                    "sexcl": sexcl, "mstk": mstk, "zpad": zpad,
                    "gfm": gfm, "den_sb": den_sb, "rden": rden,
                    "logits0": logits[0], "gates0": gates[0]}[nm]
            ncol = min(tens.shape[1], 2048)
            nrow = tens.shape[0]
            p.dma("pool", dbg_d[0:nrow, 0:ncol], tens[:, 0:ncol],
                  reads=(), writes=())
        for i in range(2):
            t0 = c * SEG + i * 128
            p.dma("pool", out_d[b, t0:t0 + 128, :],
                  x_out[:, i * D:(i + 1) * D],
                  reads=("x_out",), writes=(), tag=f"st{b}.{c}")
        p.barrier("pool")

    def run_gen(g):
        if g is None:
            return
        for _ in g:
            pass

    def inter(g1, g2, r1=1, r2=1):
        gens = [(g, r) for g, r in ((g1, r1), (g2, r2)) if g is not None]
        while gens:
            nxt_gens = []
            for g, r in gens:
                alive = True
                for _ in range(r):
                    try:
                        next(g)
                    except StopIteration:
                        alive = False
                        break
                if alive:
                    nxt_gens.append((g, r))
            gens = nxt_gens

    p.seal_narrow()
    coords = [(b, c) for b in range(n_seq) for c in range(n_chunks)]
    run_gen(phase_A1(*coords[0]))
    run_gen(phase_A2(*coords[0]))
    phase_B(*coords[0])
    for idx, (b, c) in enumerate(coords):
        nxt = coords[idx + 1] if idx + 1 < len(coords) else None
        inter(phase_C(b, c), phase_D(b, c))
        inter(phase_E(b, c), phase_A1(*nxt) if nxt else None)
        inter(phase_F(b, c), phase_A2(*nxt) if nxt else None)
        inter(phase_B_gen(*nxt) if nxt else None, phase_G2(b, c), r1=3, r2=1)

    p.final_store_wait()
    p.emit(nc)
    return p


def kernel(L0_attn_x, L0_attn_norm_w, L0_attn_Wq, L0_attn_Wk, L0_attn_Wv,
           L0_attn_Wo, L0_attn_Wu, L0_ffn_CB, L0_ffn_SCB, L0_ffn_bias,
           _n_chunks=NCH, _n_seq=2, _dump=None, _skip=()):
    x = np.ascontiguousarray(L0_attn_x, dtype=np.float32)
    wqt = np.ascontiguousarray(L0_attn_Wq.T, dtype=np.float32)
    wkt = np.ascontiguousarray(L0_attn_Wk.T, dtype=np.float32)
    wvt = np.ascontiguousarray(L0_attn_Wv.T, dtype=np.float32)
    wut = np.ascontiguousarray(L0_attn_Wu.T, dtype=np.float32)
    wot = np.ascontiguousarray(L0_attn_Wo.T, dtype=np.float32)
    cb = np.ascontiguousarray(L0_ffn_CB, dtype=np.float32)
    nw_t = np.ascontiguousarray(
        np.asarray(L0_attn_norm_w, np.float32).reshape(8, 128).T)
    scb_t = np.ascontiguousarray(
        np.asarray(L0_ffn_SCB, np.float32).reshape(8, 128).T)
    bias_r = np.ascontiguousarray(
        np.asarray(L0_ffn_bias, np.float32).reshape(1, D))

    nc = bass.Bass("TRN2", target_bir_lowering=False)
    build(nc, _n_seq, _n_chunks, dump=_dump, skip=_skip)

    in_maps = []
    for core in range(NCORES):
        im = {"x": np.ascontiguousarray(x[core * 2:core * 2 + _n_seq]),
              "wqt": wqt, "wkt": wkt, "wvt": wvt, "wut": wut, "wot": wot,
              "cb": cb, "nw": nw_t, "scb": scb_t, "bias": bias_r}
        in_maps.append(im)
    res = run_bass_kernel_spmd(nc, in_maps, core_ids=list(range(NCORES)))
    if _dump is not None:
        return res
    return np.concatenate([r["out"] for r in res.results], axis=0)


# version 22
# speedup vs baseline: 1.0120x; 1.0054x over previous
import numpy as np
import ml_dtypes
import concourse.bass as bass
import concourse.mybir as mybir
from concourse.bass_utils import run_bass_kernel_spmd

F32 = mybir.dt.float32
BF16 = mybir.dt.bfloat16
AF = mybir.ActivationFunctionType
ALU = mybir.AluOpType
AX = mybir.AxisListType

B_TOT, T, D = 16, 4096, 1024
H, DK, SEG = 16, 64, 256
NCH = T // SEG
NCORES = 8
EPS_LN = 1e-5
EPS_RMS = 1e-6
EPS_DEN = 1e-6
SCALE = 1.0 / 32.0

ENGINES = ("pe", "dve", "act", "pool", "sp")
ENG2BLK = {"pe": "tensor", "dve": "vector", "act": "scalar", "pool": "gpsimd",
           "sp": "sync"}
SELF_WAIT = True
FULL_SERIAL = False
SERIAL_ENGINES = set()
PE_WAIT_DMA = False
PE_STALL_EVERY = 0


class Prog:

    def __init__(self):
        self.streams = {e: [] for e in ENGINES}
        self.count = {e: 0 for e in ENGINES}
        self.dma_count = {"spd": 0, "gpd": 0, "spn": 0, "gpn": 0}
        self.barrier_count = {"spv": 0, "gpv": 0}
        self._sealed = False
        self.reg_w = {}
        self.reg_r = {}
        self.hwm = {e: {} for e in ENGINES}
        self.n_ops = 0

    def _wait(self, stream, semkey, val):
        if val <= self.hwm[stream].get(semkey, 0):
            return
        self.streams[stream].append(("wait", semkey, val))
        self.hwm[stream][semkey] = val

    def _edges(self, stream, reads, writes):
        deps = {}
        for r in reads:
            w = self.reg_w.get(r)
            if w is not None:
                deps[w[0]] = max(deps.get(w[0], -1), w[1])
        for w_ in writes:
            w = self.reg_w.get(w_)
            if w is not None:
                deps[w[0]] = max(deps.get(w[0], -1), w[1])
            for rr in self.reg_r.get(w_, ()):
                deps[rr[0]] = max(deps.get(rr[0], -1), rr[1])
        for semkey, val in deps.items():
            if semkey == stream:
                if stream == "pe" or not SELF_WAIT:
                    continue
                if val >= self.count[stream] + 1:
                    raise RuntimeError("self-wait on future")
            self._wait(stream, semkey, val)

    def op(self, stream, fn, reads=(), writes=(), tag=""):
        assert stream in ("pe", "dve", "act", "pool")
        if stream == "pe" and PE_WAIT_DMA:
            for dk in ("spd", "gpd"):
                if self.dma_count[dk] > 0:
                    self._wait(stream, dk, self.dma_count[dk])
        if stream == "pe" and PE_STALL_EVERY:
            self._pe_n = getattr(self, "_pe_n", 0) + 1
            if self._pe_n % PE_STALL_EVERY == 0:
                for e2 in ("dve", "act"):
                    if self.count.get(e2, 0) > 0:
                        self._wait(stream, e2, self.count[e2])
        if FULL_SERIAL or stream in SERIAL_ENGINES or SERIAL_ENGINES & set(ENGINES):
            for e2 in ENGINES:
                if e2 == stream:
                    continue
                if (FULL_SERIAL or stream in SERIAL_ENGINES or e2 in SERIAL_ENGINES) \
                        and self.count.get(e2, 0) > 0:
                    self._wait(stream, e2, self.count[e2])
            if FULL_SERIAL or stream in SERIAL_ENGINES:
                for dk in ("spd", "gpd"):
                    if self.dma_count[dk] > 0:
                        self._wait(stream, dk, self.dma_count[dk])
        self._edges(stream, reads, writes)
        self.count[stream] += 1
        cnt = self.count[stream]
        self.streams[stream].append(("op", fn, stream, 1, tag))
        for r in reads:
            self.reg_r.setdefault(r, []).append((stream, cnt))
        for w in writes:
            self.reg_w[w] = (stream, cnt)
            self.reg_r[w] = []
        self.n_ops += 1

    def dma(self, queue, out_ap, in_ap, reads=(), writes=(), tag="",
            narrow=False):
        stream = "sp" if queue == "sp" else "pool"
        if narrow:
            assert not self._sealed
            semkey = "spn" if queue == "sp" else "gpn"
            mark = None
        else:
            semkey = "spd" if queue == "sp" else "gpd"
            viskey = "spv" if queue == "sp" else "gpv"
            mark = (viskey, self.barrier_count[viskey] + 1)
        self._edges(stream, reads, writes)
        self.dma_count[semkey] += 16
        cnt = self.dma_count[semkey]

        def fn(eng, o=out_ap, i=in_ap):
            return eng.dma_start(o, i)
        self.streams[stream].append(("op", fn, semkey, 16, tag))
        if mark is None:
            mark = (semkey, cnt)
        for r in reads:
            self.reg_r.setdefault(r, []).append(mark)
        for w in writes:
            self.reg_w[w] = mark
            self.reg_r[w] = []
        self.n_ops += 1

    def barrier(self, queue):
        stream = "sp" if queue == "sp" else "pool"
        semkey = "spd" if queue == "sp" else "gpd"
        viskey = "spv" if queue == "sp" else "gpv"
        self._wait(stream, semkey, self.dma_count[semkey])

        def fn(eng):
            return eng.nop()
        self.streams[stream].append(("op", fn, viskey, 1, "barrier"))
        self.barrier_count[viskey] += 1

    def seal_narrow(self):
        self._sealed = True
        for r, w in list(self.reg_w.items()):
            if w[0] in ("spn", "gpn"):
                self.reg_w[r] = (w[0], self.dma_count[w[0]])

    def final_store_wait(self):
        self._wait("sp", "spd", self.dma_count["spd"])
        self._wait("pool", "gpd", self.dma_count["gpd"])

    def emit(self, nc):
        sems = {}
        for s in list(ENGINES) + ["spd", "gpd", "spn", "gpn", "spv", "gpv"]:
            sems[s] = nc.alloc_semaphore(f"sem_{s}")
        blk = nc.Block("main")
        with blk:
            for e in ENGINES:
                items = self.streams[e]

                def body(eng, items=items):
                    for it in items:
                        if it[0] == "wait":
                            eng.wait_ge(sems[it[1]], it[2])
                        else:
                            _, fn, semkey, inc, tag = it
                            fn(eng).then_inc(sems[semkey], inc)
                getattr(blk, ENG2BLK[e])(body)


def build(nc, n_seq, n_chunks, dump=None, skip=()):
    p = Prog()
    gen = {}

    def mm_start(key):
        if not gen.get(key, False):
            gen[key] = True
            return True
        return False

    def gen_reset(*keys):
        for k in keys:
            gen[k] = False

    x_d = nc.dram_tensor("x", [n_seq, T, D], F32, kind="ExternalInput").ap()
    out_d = nc.dram_tensor("out", [n_seq, T, D], F32, kind="ExternalOutput").ap()
    w_d = {}
    for nme in ["wqt", "wkt", "wvt", "wut", "wot"]:
        w_d[nme] = nc.dram_tensor(nme, [D, D], F32, kind="ExternalInput").ap()
    cb_d = nc.dram_tensor("cb", [D, D], F32, kind="ExternalInput").ap()
    nw_d = nc.dram_tensor("nw", [128, 8], F32, kind="ExternalInput").ap()
    scb_d = nc.dram_tensor("scb", [128, 8], F32, kind="ExternalInput").ap()
    bias_d = nc.dram_tensor("bias", [1, D], F32, kind="ExternalInput").ap()

    ident_np = np.eye(128, dtype=np.float32).astype(ml_dtypes.bfloat16)
    atm = np.triu(np.ones((128, 128), np.float32))
    atmask_np = np.concatenate([atm, np.ones((128, 128), np.float32), atm],
                               axis=1).astype(ml_dtypes.bfloat16)
    dux0_np = np.tril(np.ones((128, 128), np.float32)).astype(ml_dtypes.bfloat16)
    dux1_np = np.concatenate([np.ones((128, 128), np.float32),
                              np.tril(np.ones((128, 128), np.float32))],
                             axis=1).astype(ml_dtypes.bfloat16)
    rc_np = np.stack([1.0 / np.arange(1, 129), 1.0 / np.arange(129, 257)],
                     axis=1).astype(np.float32)
    sel_np = np.zeros((16, 16, 128), np.float32)
    for nn in range(16):
        sel_np[nn, nn, :] = 1.0
    sel_np = sel_np.astype(ml_dtypes.bfloat16)
    ones_np = np.ones((1, 512), np.float32).astype(ml_dtypes.bfloat16)

    ident_d = nc.inline_tensor(np.ascontiguousarray(ident_np), "c_ident").ap()
    atmask_d = nc.inline_tensor(atmask_np, "c_atmask").ap()
    dux0_d = nc.inline_tensor(dux0_np, "c_dux0").ap()
    dux1_d = nc.inline_tensor(dux1_np, "c_dux1").ap()
    rc_d = nc.inline_tensor(rc_np, "c_recipct").ap()
    sel_d = nc.inline_tensor(np.ascontiguousarray(sel_np.reshape(16 * 16, 128)),
                             "c_sel").ap()
    ones_d = nc.inline_tensor(ones_np, "c_ones").ap()

    A = nc.alloc_sbuf_tensor
    wsb = {nme: A(nme + "_sb", [128, 8 * D], BF16) for nme in
           ["wqt", "wkt", "wvt", "wut", "wot"]}
    cbT = A("cbT_sb", [128, 8 * D], BF16)
    arch = A("arch", [128, 8 * 960], BF16)
    xs = [A(f"xbuf{i}", [128, 2 * D], F32) for i in range(2)]
    tm_sc = A("tm_sc", [128, 2 * D], BF16)
    xln_tm = tm_sc
    xn_tm = tm_sc
    xln_fm = A("xln_fm", [128, 2 * D], BF16)
    xn_fm = A("xn_fm", [128, 2 * D], BF16)
    y_tm = A("y_tm", [128, 2 * D], BF16)
    y_fm = A("y_fm", [128, 2 * D], BF16)
    u_fm = A("u_fm", [128, 2 * D], BF16)
    pq_fm = A("pq_fm", [128, 2 * D], BF16)
    pk_fm = A("pk_fm", [128, 2 * D], BF16)
    pk_tm = A("pk_tm", [128, 2 * D], BF16)
    v_tm = A("v_tm", [128, 2 * 16 * 65], BF16)
    eluE = A("eluE", [128, 512], BF16)
    eluR = A("eluR", [128, 512], BF16)
    sexcl = A("sexcl", [128, 8 * 65], BF16)
    zpad = A("zpad", [128, 16], BF16)
    at_sb = [A(f"atsb{i}", [128, 384], BF16) for i in range(2)]
    x_out = A("x_out", [128, 2 * D], F32)
    dxm = A("dxm", [128, 256], F32)
    gsb = A("gsb", [128, 8 * 256], BF16)
    bcsb = A("bcsb", [128, 2 * 256], BF16)
    gfm = A("gfm", [16, 256], BF16)
    gates = [A(f"gates{i}", [128, 17], F32) for i in range(2)]
    logits = [A(f"logits{i}", [128, 17], F32) for i in range(2)]
    gates_bf = A("gates_bf", [128, 16], BF16)
    mstk = A("mstk", [128, 8 * 16], BF16)
    msum = A("msum", [128, 8], F32)
    ST_NAMES = ["s1", "s2a", "s2b", "s2c", "m", "var", "u1", "rstd1", "sc2",
                "b1", "b2", "q2", "rr", "mx", "negb", "z1", "rz", "scr", "lnu",
                "lnq", "rcp", "tmp"]
    st = {nme: A("st_" + nme, [128, 4], F32) for nme in ST_NAMES}
    nmr_bf = A("nmr_bf", [128, 2], BF16)
    rr_bf = A("rr_bf", [128, 2], BF16)
    rr_row = A("rr_row", [1, 256], BF16)
    nmr_row = A("nmr_row", [1, 256], BF16)
    den_sb = A("den_sb", [128, 32], F32)
    rden = A("rden", [128, 32], F32)
    ident = A("identsb", [128, 128], BF16)
    atmask = A("atmasksb", [128, 384], BF16)
    dux0 = A("dux0sb", [128, 128], BF16)
    dux1 = A("dux1sb", [128, 256], BF16)
    rcct = A("rcctsb", [128, 2], F32)
    sel = A("selsb", [16, 16 * 128], BF16)
    ones_bf = A("onessb", [1, 512], BF16)
    bias_row = A("bias_row", [1, D], BF16)
    nw_sb = A("nw_sb", [128, 8], F32)
    scb_sb = A("scb_sb", [128, 8], F32)
    wstage = [A("wstage0", [128, D], F32)[:, :], A("wstage1", [128, D], F32)[:, :]]
    cbrow = A("cbrow", [128, D], BF16)
    dbg_d = None
    if dump is not None:
        dbg_d = nc.dram_tensor("dbg", [128, 2 * D], F32,
                               kind="ExternalOutput").ap()

    P = nc.alloc_psum_tensor
    s_psA = P("s_psA", [128, 4 * 65], F32)
    s_psB = P("s_psB", [128, 4 * 65], F32)
    pa = P("pa", [128, 512], F32)
    pb = P("pb", [128, 512], F32)
    tr = P("tr", [128, 512], BF16)
    at_ps = P("at_ps", [128, 384], F32)
    ybank = P("ybank", [128, 1024], F32)

    def emit_mm(psum_region, out_ap, lhsT, rhs, reads, key=None, tp=None,
                force_start=None, tag=""):
        k = key or psum_region
        strt = mm_start(k) if force_start is None else force_start

        def fn(eng, o=out_ap, l=lhsT, r=rhs, s=strt, t=tp):
            return eng.matmul(o, l, r, start=s, stop=True, tile_position=t)
        p.op("pe", fn, reads=reads, writes=(psum_region,), tag=tag)

    def transpose(psum_ap, in_ap, reads, writes, tag=""):
        def fn(eng, o=psum_ap, i=in_ap):
            return eng.transpose(o, i, ident[:, :])
        p.op("pe", fn, reads=tuple(reads) + ("ident",), writes=writes, tag=tag)

    def act(out_ap, in_ap, func, reads, writes, scale=1.0, bias=0.0,
            accum=None, tag=""):
        def fn(eng, o=out_ap, i=in_ap, f=func, s=scale, b=bias, a=accum):
            return eng.activation(o, i, f, bias=b, scale=s, accum_out=a)
        p.op("act", fn, reads=reads, writes=writes, tag=tag)

    def act_copy(out_ap, in_ap, reads, writes, tag=""):
        def fn(eng, o=out_ap, i=in_ap):
            return eng.copy(o, i)
        p.op("act", fn, reads=reads, writes=writes, tag=tag)

    def dve(method, out_ap, *args, reads=(), writes=(), tag=""):
        def fn(eng, m=method, o=out_ap, a=args):
            return getattr(eng, m)(o, *a)
        p.op("dve", fn, reads=reads, writes=writes, tag=tag)

    def pool_op(method, out_ap, *args, reads=(), writes=(), tag=""):
        def fn(eng, m=method, o=out_ap, a=args):
            return getattr(eng, m)(o, *a)
        p.op("pool", fn, reads=reads, writes=writes, tag=tag)

    def stt(stream, out_ap, in0, scal, in1, o0, o1, reads, writes, tag=""):
        def fn(eng, o=out_ap, a=in0, s=scal, b=in1, x=o0, y=o1):
            return eng.scalar_tensor_tensor(o, a, s, b, x, y)
        p.op(stream, fn, reads=reads, writes=writes, tag=tag)

    p.dma("sp", ident[:, :], ident_d, writes=("ident",), narrow=True)
    p.dma("sp", atmask[:, :], atmask_d, writes=("atmask",), narrow=True)
    p.dma("sp", dux0[:, :], dux0_d, writes=("dux0",), narrow=True)
    p.dma("sp", dux1[:, :], dux1_d, writes=("dux1",), narrow=True)
    p.dma("sp", rcct[:, :], rc_d, writes=("rcct",), narrow=True)
    for _kt in range(16):
        p.dma("sp", sel[:, _kt * 128:(_kt + 1) * 128],
              sel_d[_kt * 16:(_kt + 1) * 16, :], writes=("sel",), narrow=True)
    p.dma("sp", ones_bf[:, :], ones_d, writes=("ones",), narrow=True)
    p.dma("sp", nw_sb[:, :], nw_d, writes=("nw",), narrow=True)
    p.dma("sp", scb_sb[:, :], scb_d, writes=("scb",), narrow=True)
    p.dma("pool", bias_row[:, :], bias_d, writes=("bias_row",), narrow=True)
    dve("memset", v_tm[:, :], 1.0, writes=("v_tm",))
    dve("memset", arch[:, :], 0.0, writes=("arch",))
    dve("memset", zpad[:, :], 0.0, writes=("zpad",))

    def load_x(b, c, buf):
        for i in range(2):
            t0 = c * SEG + i * 128
            p.dma("sp", xs[buf][:, i * D:(i + 1) * D], x_d[b, t0:t0 + 128, :],
                  reads=(f"xb{buf}_{i}",), writes=(f"xb{buf}_{i}",),
                  tag=f"ldx{b}.{c}")

    load_x(0, 0, 0)

    for wi, nme in enumerate(["wqt", "wkt", "wvt"]):
        for kt in range(8):
            sb = (wi * 8 + kt) % 2
            p.dma("sp", wstage[sb], w_d[nme][kt * 128:(kt + 1) * 128, :],
                  reads=(f"wstage{sb}",), writes=(f"wstage{sb}",))
            if sb == 1:
                p.barrier("sp")
            dve("tensor_scalar_mul", wsb[nme][:, kt * D:(kt + 1) * D],
                wstage[sb], nw_sb[:, kt:kt + 1],
                reads=(f"wstage{sb}", "nw"), writes=(f"{nme}_k{kt}",))
    for nme in ["wut", "wot"]:
        for kt in range(8):
            p.dma("pool", wsb[nme][:, kt * D:(kt + 1) * D],
                  w_d[nme][kt * 128:(kt + 1) * 128, :],
                  writes=(f"{nme}_k{kt}",))
    p.barrier("pool")
    for nt in range(8):
        sb = nt % 2
        p.dma("sp", wstage[sb], cb_d[nt * 128:(nt + 1) * 128, :],
              reads=(f"wstage{sb}",), writes=(f"wstage{sb}",))
        p.barrier("sp")
        def cbs_fn(eng, o=cbrow[:, :], i=wstage[sb],
                   s1=scb_sb[:, nt:nt + 1]):
            return eng.tensor_scalar(o, i, s1, 1.0 / 127.0, ALU.mult, ALU.mult)
        p.op("dve", cbs_fn, reads=(f"wstage{sb}", "scb"), writes=("cbrow",))
        for k2 in (0, 4):
            for kk in range(4):
                kt = k2 + kk
                transpose(tr[:, kk * 128:(kk + 1) * 128],
                          cbrow[:, kt * 128:(kt + 1) * 128],
                          reads=("cbrow",), writes=("tr0", "tr1"))
            dst = cbT[:, :].rearrange("p (k g) -> p k g", k=8)[
                :, k2:k2 + 4, nt * 128:(nt + 1) * 128]
            src = tr[:, :].rearrange("p (k g) -> p k g", k=4)
            act_copy(dst, src, reads=("tr0", "tr1"), writes=("cbT",))

    def ln_stats2(x_aps, xregs, stc, eps, rms, sq_target, sq_region, pieces):
        c2 = (slice(None), slice(stc, stc + 2))
        accs = ["s2a", "s2b", "s2c"]
        for i in range(2):
            dve("tensor_reduce", st["s1"][:, stc + i:stc + i + 1], x_aps[i],
                AX.X, ALU.add, reads=(xregs[i],), writes=("st_s1",))
            off = 0
            for pi, plen in enumerate(pieces):
                act(sq_target[:, 0:plen], x_aps[i][:, off:off + plen],
                    AF.Square, reads=(xregs[i],), writes=(sq_region,),
                    accum=st[accs[pi]][:, stc + i:stc + i + 1])
                p.reg_w["st_" + accs[pi]] = p.reg_w[sq_region]
                p.reg_r["st_" + accs[pi]] = []
                off += plen
        dve("tensor_scalar_mul", st["m"][c2], st["s1"][c2], 1.0 / 1024.0,
            reads=("st_s1",), writes=("st_m",))
        dve("tensor_add", st["tmp"][c2], st["s2a"][c2], st["s2b"][c2],
            reads=("st_s2a", "st_s2b"), writes=("st_tmp",))
        if len(pieces) > 2:
            dve("tensor_add", st["tmp"][c2], st["tmp"][c2], st["s2c"][c2],
                reads=("st_tmp", "st_s2c"), writes=("st_tmp",))
        dve("tensor_mul", st["q2"][c2], st["m"][c2], st["m"][c2],
            reads=("st_m",), writes=("st_q2",))
        stt("dve", st["var"][c2], st["tmp"][c2], 1.0 / 1024.0, st["q2"][c2],
            ALU.mult, ALU.subtract, ("st_tmp", "st_q2"), ("st_var",))
        dve("tensor_scalar_add", st["u1"][c2], st["var"][c2], float(eps),
            reads=("st_var",), writes=("st_u1",))
        act(st["lnu"][c2], st["u1"][c2], AF.Ln,
            reads=("st_u1",), writes=("st_lnu",))
        act(st["rstd1"][c2], st["lnu"][c2], AF.Exp, scale=-0.5,
            reads=("st_lnu",), writes=("st_rstd1",))
        stt("dve", st["b1"][c2], st["m"][c2], -1.0, st["rstd1"][c2],
            ALU.mult, ALU.mult, ("st_m", "st_rstd1"), ("st_b1",))
        if rms:
            dve("tensor_mul", st["rcp"][c2], st["rstd1"][c2], st["rstd1"][c2],
                reads=("st_rstd1",), writes=("st_rcp",))
            def q2f(eng, o=st["q2"][c2], x=st["rcp"][c2]):
                return eng.tensor_scalar(o, x, -float(eps), 1.0 + EPS_RMS,
                                         ALU.mult, ALU.add)
            p.op("dve", q2f, reads=("st_rcp",), writes=("st_q2",))
            act(st["lnq"][c2], st["q2"][c2], AF.Ln, reads=("st_q2",),
                writes=("st_lnq",))
            act(st["rr"][c2], st["lnq"][c2], AF.Exp, scale=-0.5,
                reads=("st_lnq",), writes=("st_rr",))
            dve("tensor_mul", st["sc2"][c2], st["rstd1"][c2], st["rr"][c2],
                reads=("st_rstd1", "st_rr"), writes=("st_sc2",))
            stt("dve", st["b2"][c2], st["m"][c2], -1.0, st["sc2"][c2],
                ALU.mult, ALU.mult, ("st_m", "st_sc2"), ("st_b2",))

    def transpose_set(src, src_region, dst, dst_region, direction,
                      accum_ms=False):
        if direction == "tm2fm":
            tiles = [(i * 1024 + j * 128, j * 256 + i * 128)
                     for j in range(8) for i in range(2)]
        else:
            tiles = [(g * 256 + i * 128, i * 1024 + g * 128)
                     for i in range(2) for g in range(8)]
        for half in range(4):
            for kk in range(4):
                s_off, _ = tiles[half * 4 + kk]
                transpose(tr[:, kk * 128:(kk + 1) * 128],
                          src[:, s_off:s_off + 128],
                          reads=(src_region,), writes=("tr0", "tr1"))
            if accum_ms:
                for jj in range(2):
                    j = half * 2 + jj
                    d_off = tiles[half * 4 + jj * 2][1]
                    act(dst[:, d_off:d_off + 256],
                        tr[:, jj * 256:(jj + 1) * 256], AF.Copy,
                        reads=("tr0", "tr1"), writes=(dst_region,),
                        accum=msum[:, j:j + 1])
                    p.reg_w[f"msum_{j}"] = p.reg_w[dst_region]
                    p.reg_r[f"msum_{j}"] = []
            else:
                d_off = tiles[half * 4][1]
                if half % 2 == 0:
                    act_copy(dst[:, d_off:d_off + 512], tr[:, 0:512],
                             reads=("tr0", "tr1"), writes=(dst_region,))
                else:
                    dve("tensor_copy", dst[:, d_off:d_off + 512], tr[:, 0:512],
                        reads=("tr0", "tr1"), writes=(dst_region,))
            yield

    def proj_fm(wname, epilogue, src_fm, src_region):
        for w in range(4):
            ps, psr = (pa, "pa") if w % 2 == 0 else (pb, "pb")
            gen_reset(psr)
            for half in range(2):
                ot = 2 * w + half
                for kt in range(8):
                    emit_mm(psr, ps[:, half * 256:(half + 1) * 256],
                            wsb[wname][:, kt * D + ot * 128:kt * D + (ot + 1) * 128],
                            src_fm[:, kt * 256:(kt + 1) * 256],
                            reads=(f"{wname}_k{kt}", src_region), key=psr)
            epilogue(w, ps, psr)
            yield

    def phase_A1(b, c):
        buf = c % 2
        xreg = (f"xb{buf}_0", f"xb{buf}_1")
        stc = (c % 2) * 2
        if c == 0:
            dve("memset", gfm[:, :], 0.0, writes=("gfm",))
            gen_reset("s_psA0", "s_psA64", "s_psB0", "s_psB64")
        p.barrier("sp")
        if c + 1 < n_chunks:
            load_x(b, c + 1, (c + 1) % 2)
        elif b + 1 < n_seq:
            load_x(b + 1, 0, (c + 1) % 2)
        x_aps = [xs[buf][:, i * D:(i + 1) * D] for i in range(2)]
        ln_stats2(x_aps, xreg, stc, EPS_LN, True, at_ps, "at_ps",
                  (384, 384, 256))
        yield
        for i in range(2):
            def aff(eng, o=xln_tm[:, i * D:(i + 1) * D], x=x_aps[i],
                    s1=st["rstd1"][:, stc + i:stc + i + 1],
                    s2=st["b1"][:, stc + i:stc + i + 1]):
                return eng.tensor_scalar(o, x, s1, s2, ALU.mult, ALU.add)
            p.op("dve", aff, reads=(xreg[i], "st_rstd1", "st_b1"),
                 writes=("tm_sc",))
            act_copy(nmr_bf[:, i:i + 1], st["b1"][:, stc + i:stc + i + 1],
                     reads=("st_b1",), writes=("nmr_bf",))
            yield

    def phase_A2(b, c):
        buf = c % 2
        xreg = (f"xb{buf}_0", f"xb{buf}_1")
        stc = (c % 2) * 2
        yield from transpose_set(xln_tm, "tm_sc", xln_fm, "xln_fm", "tm2fm",
                                 accum_ms=(c < n_chunks - 1))
        if c < n_chunks - 1:
            for j in range(8):
                dve("tensor_scalar_mul", mstk[:, j * 16 + c:j * 16 + c + 1],
                    msum[:, j:j + 1], 1.0 / 256.0,
                    reads=(f"msum_{j}",), writes=("mstk",))
        act_copy(rr_bf[:, 0:2], st["rr"][:, stc:stc + 2], reads=("st_rr",),
                 writes=("rr_bf",))
        for i in range(2):
            transpose(tr[0:1, i * 128:(i + 1) * 128], nmr_bf[:, i:i + 1],
                      reads=("nmr_bf",), writes=("tr0", "tr1"))
            act_copy(nmr_row[0:1, i * 128:(i + 1) * 128],
                     tr[0:1, i * 128:(i + 1) * 128],
                     reads=("tr0", "tr1"), writes=("nmr_row",))
            transpose(tr[0:1, 256 + i * 128:256 + (i + 1) * 128],
                      rr_bf[:, i:i + 1], reads=("rr_bf",),
                      writes=("tr0", "tr1"))
            act_copy(rr_row[0:1, i * 128:(i + 1) * 128],
                     tr[0:1, 256 + i * 128:256 + (i + 1) * 128],
                     reads=("tr0", "tr1"), writes=("rr_row",))
        gen_reset("pa")
        emit_mm("pa", pa[:, 0:256], ones_bf[0:1, 0:128], rr_row[0:1, 0:256],
                reads=("ones", "rr_row"), key="pa")
        xl3 = xln_fm[:, :].rearrange("p (g t) -> p g t", g=8)
        bc3 = pa[:, 0:256].unsqueeze(1).broadcast_to([128, 8, 256])
        dve("tensor_mul", xn_fm[:, :].rearrange("p (g t) -> p g t", g=8),
            xl3, bc3, reads=("xln_fm", "pa"), writes=("xn_fm",))
        yield

    def phase_B_gen(b, c):
        yield from _phase_B(b, c)

    def phase_B(b, c):
        for _ in _phase_B(b, c):
            pass

    def _phase_B(b, c):
        stc = (c % 2) * 2

        def u_epi(w, ps, psr):
            act_copy(u_fm[:, w * 512:(w + 1) * 512], ps[:, :], reads=(psr,),
                     writes=("u_fm",))

        def elu_epi_for(dst, dstr):
            def elu_epi(w, ps, psr):
                act(eluE[:, :], ps[:, :], AF.Exp, reads=(psr,),
                    writes=("eluE",))
                act(eluR[:, :], ps[:, :], AF.Relu, reads=(psr,),
                    writes=("eluR",))
                stt("dve", dst[:, w * 512:(w + 1) * 512], eluE[:, :], 1.0,
                    eluR[:, :], ALU.min, ALU.add, ("eluE", "eluR"), (dstr,))
            return elu_epi

        yield from proj_fm("wut", u_epi, xln_fm, "xln_fm")
        yield from proj_fm("wqt", elu_epi_for(pq_fm, "pq_fm"), xn_fm, "xn_fm")
        yield from proj_fm("wkt", elu_epi_for(pk_fm, "pk_fm"), xn_fm, "xn_fm")
        for i in range(2):
            for half in range(2):
                ps, psr = (pa, "pa") if half == 0 else (pb, "pb")
                gen_reset(psr)
                for kt in range(8):
                    emit_mm(psr, ps[:, :],
                            xn_fm[:, kt * 256 + i * 128:kt * 256 + (i + 1) * 128],
                            wsb["wvt"][:, kt * D + half * 512:kt * D + (half + 1) * 512],
                            reads=(f"wvt_k{kt}", "xn_fm"), key=psr)
                dst = v_tm[:, :].rearrange("p (ii h vv) -> p ii h vv",
                                           ii=2, h=16)[:, i,
                                                       half * 8:(half + 1) * 8,
                                                       0:64]
                src = ps[:, :].rearrange("p (h vv) -> p h vv", h=8)
                act_copy(dst, src, reads=(psr,), writes=("v_tm",))
                yield
        for i in range(2):
            ps, psr = (pa, "pa") if i == 0 else (pb, "pb")
            gen_reset(psr)
            ncols = 128 if i == 0 else 256
            for kt in range(8):
                emit_mm(psr, ps[:, 0:ncols],
                        u_fm[:, kt * 256 + i * 128:kt * 256 + (i + 1) * 128],
                        xln_fm[:, kt * 256:kt * 256 + ncols],
                        reads=("u_fm", "xln_fm"), key=psr)
            mask, mr = (dux0, "dux0") if i == 0 else (dux1, "dux1")
            dve("tensor_mul", dxm[:, 0:ncols], ps[:, 0:ncols],
                mask[:, 0:ncols], reads=(psr, mr), writes=("dxm",))
            dve("tensor_reduce", st["mx"][:, stc + i:stc + i + 1],
                dxm[:, 0:ncols], AX.X, ALU.add, reads=("dxm",),
                writes=("st_mx",))
            dve("tensor_scalar_mul", logits[i][:, c:c + 1],
                st["mx"][:, stc + i:stc + i + 1], rcct[:, i:i + 1],
                reads=("st_mx", "rcct"), writes=(f"logits{i}",))
            yield
        if c > 0:
            for i in range(2):
                ps, psr = (pa, "pa") if i == 0 else (pb, "pb")
                gen_reset(psr)
                for kt in range(8):
                    emit_mm(psr, ps[:, 0:c],
                            u_fm[:, kt * 256 + i * 128:kt * 256 + (i + 1) * 128],
                            mstk[:, kt * 16:kt * 16 + c],
                            reads=("u_fm", "mstk"), key=psr)
                act_copy(logits[i][:, 0:c], ps[:, 0:c], reads=(psr,),
                         writes=(f"logits{i}",))
        for i in range(2):
            lr, gr = f"logits{i}", f"gates{i}"
            ci = (slice(None), slice(stc + i, stc + i + 1))
            dve("tensor_reduce", st["mx"][ci], logits[i][:, 0:c + 1],
                AX.X, ALU.max, reads=(lr,), writes=("st_mx",))
            dve("tensor_scalar_mul", st["negb"][ci], st["mx"][ci], -SCALE,
                reads=("st_mx",), writes=("st_negb",))
            act(gates[i][:, 0:c + 1], logits[i][:, 0:c + 1], AF.Exp,
                scale=SCALE, bias=st["negb"][ci],
                reads=(lr, "st_negb"), writes=(gr,))
            dve("tensor_reduce", st["z1"][ci], gates[i][:, 0:c + 1],
                AX.X, ALU.add, reads=(gr,), writes=("st_z1",))
            dve("reciprocal", st["rz"][ci], st["z1"][ci],
                reads=("st_z1",), writes=("st_rz",))
            dve("tensor_scalar_mul", gates[i][:, 0:c + 1],
                gates[i][:, 0:c + 1], st["rz"][ci],
                reads=(gr, "st_rz"), writes=(gr,))

    def phase_C(b, c):
        if c > 0:
            act_copy(sexcl[:, 0:260], s_psA[:, :], reads=("s_psA",),
                     writes=("sexcl",))
            act_copy(sexcl[:, 260:520], s_psB[:, :], reads=("s_psB",),
                     writes=("sexcl",))
            for (sps, spr, h0) in ((s_psA, "s_psA", 0), (s_psB, "s_psB", 8)):
                for rw in (0, 64):
                    src_ap = sps[rw:rw + 64, :].rearrange(
                        "p (g q) -> p g q", g=4)[:, :, 64:65]
                    dst_ap = zpad[rw:rw + 64, h0 + rw // 64:h0 + 8:2].unsqueeze(2)
                    act_copy(dst_ap, src_ap, reads=(spr,), writes=("zpad",))
        yield
        last = c >= n_chunks - 1
        for gp in range(4):
            for kk, (g, i) in enumerate([(2 * gp, 0), (2 * gp, 1),
                                         (2 * gp + 1, 0), (2 * gp + 1, 1)]):
                transpose(tr[:, kk * 128:(kk + 1) * 128],
                          pk_fm[:, g * 256 + i * 128:g * 256 + (i + 1) * 128],
                          reads=("pk_fm",), writes=("tr0", "tr1"))
            act_copy(pk_tm[:, 2 * gp * 128:2 * gp * 128 + 256
                           ].rearrange("p (k f) -> p k f", k=2),
                     tr[:, :].rearrange("p (k f) -> p k f", k=4)[:, 0::2, :],
                     reads=("tr0", "tr1"), writes=("pk_tm", "trtok"))
            dve("tensor_copy",
                pk_tm[:, 1024 + 2 * gp * 128:1024 + 2 * gp * 128 + 256
                      ].rearrange("p (k f) -> p k f", k=2),
                tr[:, :].rearrange("p (k f) -> p k f", k=4)[:, 1::2, :],
                reads=("tr0", "tr1", "trtok"), writes=("pk_tm",))
            yield
            if not last:
                for h in range(4 * gp, 4 * gp + 4):
                    hp = h // 2
                    sps, spr = (s_psA, "s_psA") if hp < 4 else (s_psB, "s_psB")
                    col = (hp % 4) * 65
                    rw = (h % 2) * 64
                    for i in range(2):
                        emit_mm(spr, sps[rw:rw + 64, col:col + 65],
                                pk_tm[:, i * D + h * 64:i * D + (h + 1) * 64],
                                v_tm[:, i * 1040 + h * 65:i * 1040 + (h + 1) * 65],
                                reads=("pk_tm", "v_tm"), key=f"{spr}{rw}",
                                tp=(0, rw))
                    if h % 2 == 1:
                        yield
        if not last:
            for (sps, spr, hp0) in ((s_psA, "s_psA", 0), (s_psB, "s_psB", 4)):
                src_ap = sps[:, :].rearrange("p (g q) -> p g q", g=4)[:, :, 0:64]
                dst_ap = arch[:, :].rearrange("p (g q) -> p g q", g=8)[
                    :, hp0:hp0 + 4, c * 64:(c + 1) * 64]
                act_copy(dst_ap, src_ap, reads=(spr,), writes=("arch",))
            yield

    def phase_D(b, c):
        gen_reset("den")

        def emit_m4(h):
            hp = h // 2
            rw = (h % 2) * 64
            aps, apr = (at_ps, "at_ps") if h % 2 == 0 else (pb, "pb")
            gen_reset(apr)
            emit_mm(apr, aps[:, 0:256],
                    pk_fm[rw:rw + 64, hp * 256:hp * 256 + 128],
                    pq_fm[rw:rw + 64, hp * 256:(hp + 1) * 256],
                    reads=("pk_fm", "pq_fm"), key=apr, tp=(rw, 0))
            emit_mm(apr, aps[:, 256:384],
                    pk_fm[rw:rw + 64, hp * 256 + 128:(hp + 1) * 256],
                    pq_fm[rw:rw + 64, hp * 256 + 128:(hp + 1) * 256],
                    reads=("pk_fm", "pq_fm"), key=apr, tp=(rw, 0))

        emit_m4(0)
        for hhalf in range(2):
            gen_reset("ybA", "ybB")
            for h8 in range(8):
                h = hhalf * 8 + h8
                if h < 15:
                    emit_m4(h + 1)
                hp = h // 2
                rw = (h % 2) * 64
                slot = at_sb[h % 2]
                slotr = f"atsb{h % 2}"
                aps, apr = (at_ps, "at_ps") if h % 2 == 0 else (pb, "pb")
                dve("tensor_mul", slot[:, 0:128], aps[:, 0:128],
                    atmask[:, 0:128], reads=(apr, "atmask"), writes=(slotr,))
                dve("tensor_mul", slot[:, 128:384], aps[:, 128:384],
                    atmask[:, 128:384], reads=(apr, "atmask"),
                    writes=(slotr + "x",))
                par = h8 % 2
                ybr = "ybA" if par == 0 else "ybB"
                ktiles = {0: [(0, slice(0, 128))],
                          1: [(0, slice(128, 256)), (1, slice(256, 384))]}
                for i in range(2):
                    ycol = par * 512 + i * 256 + (h8 // 2) * 64
                    for (dt, sl) in ktiles[i]:
                        slr = slotr if sl.start == 0 else slotr + "x"
                        emit_mm("ybank", ybank[:, ycol:ycol + 64],
                                slot[:, sl],
                                v_tm[:, dt * 1040 + h * 65:dt * 1040 + h * 65 + 64],
                                reads=(slr, "v_tm"), key=ybr)
                        emit_mm("pa", pa[:, i * 16 + h:i * 16 + h + 1],
                                slot[:, sl],
                                v_tm[:, dt * 1040 + h * 65 + 64:dt * 1040 + h * 65 + 65],
                                reads=(slr, "v_tm"), key="den")
                    if c > 0:
                        sbase = (hp % 4) * 65 + 260 * (hp // 4)
                        emit_mm("ybank", ybank[:, ycol:ycol + 64],
                                pq_fm[rw:rw + 64, hp * 256 + i * 128:hp * 256 + (i + 1) * 128],
                                sexcl[rw:rw + 64, sbase:sbase + 64],
                                reads=("pq_fm", "sexcl"), key=ybr,
                                tp=(rw, 0))
                        emit_mm("pa", pa[:, i * 16 + h:i * 16 + h + 1],
                                pq_fm[:, hp * 256 + i * 128:hp * 256 + (i + 1) * 128],
                                zpad[:, h:h + 1],
                                reads=("pq_fm", "zpad"), key="den")
                yield
            for i in range(2):
                cl = i * 16 + hhalf * 8
                act(den_sb[:, cl:cl + 8], pa[:, cl:cl + 8], AF.Copy,
                    bias=EPS_DEN, reads=("pa",), writes=("den_sb",))
                dve("reciprocal", rden[:, cl:cl + 8], den_sb[:, cl:cl + 8],
                    reads=("den_sb",), writes=("rden",))
                for par in range(2):
                    y3 = ybank[:, par * 512 + i * 256:par * 512 + (i + 1) * 256
                               ].rearrange("p (h vv) -> p h vv", h=4)
                    rv = rden[:, cl + par:cl + 8:2].unsqueeze(2).broadcast_to(
                        [128, 4, 64])
                    dst = y_tm[:, i * D + hhalf * 512:i * D + (hhalf + 1) * 512
                               ].rearrange("p (h vv) -> p h vv", h=8)[:, par::2, :]
                    dve("tensor_mul", dst, y3, rv,
                        reads=("ybank", "rden"), writes=("y_tm",))
            tiles = [(i * 1024 + j * 128, j * 256 + i * 128)
                     for j in range(hhalf * 4, hhalf * 4 + 4)
                     for i in range(2)]
            for half in range(2):
                for kk in range(4):
                    s_off, _ = tiles[half * 4 + kk]
                    transpose(tr[:, kk * 128:(kk + 1) * 128],
                              y_tm[:, s_off:s_off + 128],
                              reads=("y_tm",), writes=("tr0", "tr1"))
                d_off = tiles[half * 4][1]
                if half == 0:
                    act_copy(y_fm[:, d_off:d_off + 512], tr[:, 0:512],
                             reads=("tr0", "tr1"), writes=("y_fm",))
                else:
                    dve("tensor_copy", y_fm[:, d_off:d_off + 512],
                        tr[:, 0:512], reads=("tr0", "tr1"), writes=("y_fm",))
                yield

    def phase_E(b, c):
        buf = c % 2
        xreg = (f"xb{buf}_0", f"xb{buf}_1")
        stc = (c % 2) * 2
        for i in range(2):
            for half in range(2):
                ps, psr = (pa, "pa") if half == 0 else (pb, "pb")
                gen_reset(psr)
                for kt in range(8):
                    emit_mm(psr, ps[:, :],
                            y_fm[:, kt * 256 + i * 128:kt * 256 + (i + 1) * 128],
                            wsb["wot"][:, kt * D + half * 512:kt * D + (half + 1) * 512],
                            reads=(f"wot_k{kt}", "y_fm"), key=psr)
                emit_mm(psr, ps[:, :],
                        nmr_row[0:1, i * 128:(i + 1) * 128],
                        ones_bf[0:1, 0:512],
                        reads=("nmr_row", "ones"), key=psr)
                stt("dve", x_out[:, i * D + half * 512:i * D + (half + 1) * 512],
                    xs[buf][:, i * D + half * 512:i * D + (half + 1) * 512],
                    st["rstd1"][:, stc + i:stc + i + 1], ps[:, :],
                    ALU.mult, ALU.add,
                    (xreg[i], "st_rstd1", psr), ("x_out",))
                yield

    def phase_F(b, c):
        buf = c % 2
        xreg = (f"xb{buf}_0", f"xb{buf}_1")
        if c > 0:
            for i in range(2):
                act_copy(gates_bf[:, 0:c], gates[i][:, 0:c],
                         reads=(f"gates{i}",), writes=("gates_bf",))
                transpose(tr[0:c, i * 128:(i + 1) * 128], gates_bf[:, 0:c],
                          reads=("gates_bf",), writes=("tr0",))
                act_copy(gfm[0:c, i * 128:(i + 1) * 128],
                         tr[0:c, i * 128:(i + 1) * 128],
                         reads=("tr0",), writes=("gfm",))
            yield
            for hhalf in range(2):
                gen_reset("ybA", "ybB")

                def emit_bc_gtt(nn):
                    bslot = nn % 2
                    gen_reset("pa")
                    emit_mm("pa", pa[:, 0:256],
                            sel[:, nn * 128:(nn + 1) * 128], gfm[:, :],
                            reads=("sel", "gfm"), key="pa")
                    act_copy(bcsb[:, bslot * 256:(bslot + 1) * 256],
                             pa[:, 0:256], reads=("pa",),
                             writes=(f"bcsb{bslot}",))
                    for hp4 in range(4):
                        hp = hhalf * 4 + hp4
                        gslot = bslot * 4 + hp4
                        dve("tensor_mul",
                            gsb[:, gslot * 256:(gslot + 1) * 256],
                            pq_fm[:, hp * 256:(hp + 1) * 256],
                            bcsb[:, bslot * 256:(bslot + 1) * 256],
                            reads=("pq_fm", f"bcsb{bslot}"),
                            writes=(f"gsb{gslot}",))

                emit_bc_gtt(0)
                for nn in range(c):
                    if nn + 1 < c:
                        emit_bc_gtt(nn + 1)
                    bslot = nn % 2
                    for i in range(2):
                        for h8 in range(8):
                            h = hhalf * 8 + h8
                            hp4 = h8 // 2
                            hp = hhalf * 4 + hp4
                            gslot = bslot * 4 + hp4
                            rw = (h % 2) * 64
                            par = h8 % 2
                            ybr = "ybA" if par == 0 else "ybB"
                            ycol = par * 512 + i * 256 + (h8 // 2) * 64
                            emit_mm("ybank",
                                    ybank[:, ycol:ycol + 64],
                                    gsb[rw:rw + 64,
                                        gslot * 256 + i * 128:gslot * 256 + (i + 1) * 128],
                                    arch[rw:rw + 64,
                                         hp * 960 + nn * 64:hp * 960 + (nn + 1) * 64],
                                    reads=(f"gsb{gslot}", "arch"), key=ybr,
                                    tp=(rw, 0))
                    yield
                for i in range(2):
                    for par in range(2):
                        xo3 = x_out[:, i * D + hhalf * 512:i * D + (hhalf + 1) * 512
                                    ].rearrange("p (h vv) -> p h vv", h=8)[:, par::2, :]
                        yb3 = ybank[:, par * 512 + i * 256:par * 512 + (i + 1) * 256
                                    ].rearrange("p (h vv) -> p h vv", h=4)
                        stt("dve", xo3, xo3, gates[i][:, c:c + 1], yb3,
                            ALU.mult, ALU.add,
                            ("x_out", f"gates{i}", "ybank"), ("x_out",))
        else:
            for i in range(2):
                dve("tensor_scalar_mul", x_out[:, i * D:(i + 1) * D],
                    x_out[:, i * D:(i + 1) * D], gates[i][:, c:c + 1],
                    reads=("x_out", f"gates{i}"), writes=("x_out",))
        for i in range(2):
            pool_op("tensor_add", x_out[:, i * D:(i + 1) * D],
                    x_out[:, i * D:(i + 1) * D],
                    xs[buf][:, i * D:(i + 1) * D],
                    reads=("x_out", xreg[i]), writes=("x_out",))
        yield

    def phase_G2(b, c):
        stc = (c % 2) * 2
        ln_stats2([x_out[:, i * D:(i + 1) * D] for i in range(2)],
                  ("x_out", "x_out"), stc, EPS_LN, False, ybank, "ybank",
                  (512, 512))
        for i in range(2):
            def aff3(eng, o=y_tm[:, i * D:(i + 1) * D],
                     x=x_out[:, i * D:(i + 1) * D],
                     s1=st["rstd1"][:, stc + i:stc + i + 1],
                     s2=st["b1"][:, stc + i:stc + i + 1]):
                return eng.tensor_scalar(o, x, s1, s2, ALU.mult, ALU.add)
            p.op("dve", aff3, reads=("x_out", "st_rstd1", "st_b1"),
                 writes=("y_tm",))
        yield
        yield from transpose_set(y_tm, "y_tm", y_fm, "y_fm", "tm2fm")
        for i in range(2):
            for half in range(2):
                ps = ybank[:, half * 512:(half + 1) * 512]
                key = "ybA" if half == 0 else "ybB"
                gen_reset(key)
                for kt in range(8):
                    emit_mm("ybank", ps,
                            y_fm[:, kt * 256 + i * 128:kt * 256 + (i + 1) * 128],
                            cbT[:, kt * D + half * 512:kt * D + (half + 1) * 512],
                            reads=("cbT", "y_fm"), key=key)
                emit_mm("ybank", ps, ones_bf[0:1, 0:128],
                        bias_row[0:1, half * 512:(half + 1) * 512],
                        reads=("ones", "bias_row"), key=key)
                dve("tensor_add",
                    x_out[:, i * D + half * 512:i * D + (half + 1) * 512],
                    x_out[:, i * D + half * 512:i * D + (half + 1) * 512],
                    ps, reads=("x_out", "ybank"), writes=("x_out",))
                yield
        if dump is not None and dump[1] == (b, c):
            nm = dump[0]
            tens = {"xln_fm": xln_fm, "xn_fm": xn_fm, "u_fm": u_fm,
                    "pq_fm": pq_fm, "pk_fm": pk_fm, "pk_tm": pk_tm,
                    "v_tm": v_tm, "y_tm": y_tm, "y_fm": y_fm,
                    "x_out": x_out, "bs": x_out, "arch": arch,
                    "sexcl": sexcl, "mstk": mstk, "zpad": zpad,
                    "gfm": gfm, "den_sb": den_sb, "rden": rden,
                    "logits0": logits[0], "gates0": gates[0]}[nm]
            ncol = min(tens.shape[1], 2048)
            nrow = tens.shape[0]
            p.dma("pool", dbg_d[0:nrow, 0:ncol], tens[:, 0:ncol],
                  reads=(), writes=())
        for i in range(2):
            t0 = c * SEG + i * 128
            p.dma("pool", out_d[b, t0:t0 + 128, :],
                  x_out[:, i * D:(i + 1) * D],
                  reads=("x_out",), writes=(), tag=f"st{b}.{c}")
        p.barrier("pool")

    def run_gen(g):
        if g is None:
            return
        for _ in g:
            pass

    def inter(g1, g2, r1=1, r2=1):
        gens = [(g, r) for g, r in ((g1, r1), (g2, r2)) if g is not None]
        while gens:
            nxt_gens = []
            for g, r in gens:
                alive = True
                for _ in range(r):
                    try:
                        next(g)
                    except StopIteration:
                        alive = False
                        break
                if alive:
                    nxt_gens.append((g, r))
            gens = nxt_gens

    p.seal_narrow()
    coords = [(b, c) for b in range(n_seq) for c in range(n_chunks)]
    run_gen(phase_A1(*coords[0]))
    run_gen(phase_A2(*coords[0]))
    phase_B(*coords[0])
    for idx, (b, c) in enumerate(coords):
        nxt = coords[idx + 1] if idx + 1 < len(coords) else None
        inter(phase_C(b, c), phase_D(b, c))
        inter(phase_E(b, c), phase_A1(*nxt) if nxt else None)
        inter(phase_F(b, c), phase_A2(*nxt) if nxt else None)
        inter(phase_B_gen(*nxt) if nxt else None, phase_G2(b, c), r1=3, r2=1)

    p.final_store_wait()
    p.emit(nc)
    return p


def kernel(L0_attn_x, L0_attn_norm_w, L0_attn_Wq, L0_attn_Wk, L0_attn_Wv,
           L0_attn_Wo, L0_attn_Wu, L0_ffn_CB, L0_ffn_SCB, L0_ffn_bias,
           _n_chunks=NCH, _n_seq=2, _dump=None, _skip=()):
    x = np.ascontiguousarray(L0_attn_x, dtype=np.float32)
    wqt = np.ascontiguousarray(L0_attn_Wq.T, dtype=np.float32)
    wkt = np.ascontiguousarray(L0_attn_Wk.T, dtype=np.float32)
    wvt = np.ascontiguousarray(L0_attn_Wv.T, dtype=np.float32)
    wut = np.ascontiguousarray(L0_attn_Wu.T, dtype=np.float32)
    wot = np.ascontiguousarray(L0_attn_Wo.T, dtype=np.float32)
    cb = np.ascontiguousarray(L0_ffn_CB, dtype=np.float32)
    nw_t = np.ascontiguousarray(
        np.asarray(L0_attn_norm_w, np.float32).reshape(8, 128).T)
    scb_t = np.ascontiguousarray(
        np.asarray(L0_ffn_SCB, np.float32).reshape(8, 128).T)
    bias_r = np.ascontiguousarray(
        np.asarray(L0_ffn_bias, np.float32).reshape(1, D))

    nc = bass.Bass("TRN2", target_bir_lowering=False)
    build(nc, _n_seq, _n_chunks, dump=_dump, skip=_skip)

    in_maps = []
    for core in range(NCORES):
        im = {"x": np.ascontiguousarray(x[core * 2:core * 2 + _n_seq]),
              "wqt": wqt, "wkt": wkt, "wvt": wvt, "wut": wut, "wot": wot,
              "cb": cb, "nw": nw_t, "scb": scb_t, "bias": bias_r}
        in_maps.append(im)
    res = run_bass_kernel_spmd(nc, in_maps, core_ids=list(range(NCORES)))
    if _dump is not None:
        return res
    return np.concatenate([r["out"] for r in res.results], axis=0)
